# version 01
# baseline (speedup 1.0000x reference)
import numpy as np
import ml_dtypes
from concourse.bass_utils import run_bass_kernel_spmd

"""Block program builders (developed standalone, later fused into kernel.py).

Conventions:
- One core = one batch element (T=4096 tokens, D=1024).
- Stream token-major [T, D] fp32 in HBM (v1 unfused).
- Engines emitted interleaved in dataflow order; cross-engine order via
  semaphores with build-time counters (p.c). Same-engine DVE RAW pairs need
  >=1 independent DVE op in between (verified hardware hazard).
- Matmuls: bf16 operands, fp32 PSUM accumulation.
"""
import contextlib
import numpy as np
import concourse.bass as bass
import concourse.mybir as mybir

F32 = mybir.dt.float32
BF16 = mybir.dt.bfloat16
I32 = mybir.dt.int32
AF = mybir.ActivationFunctionType
ALU = mybir.AluOpType
AX = mybir.AxisListType

T = 4096
D = 1024
NT = T // 128


class P:
    def __init__(self):
        self.nc = bass.Bass("TRN2", target_bir_lowering=False)
        self.es = contextlib.ExitStack()
        self.c = {}

    def sbuf(self, name, shape, dt):
        return self.es.enter_context(self.nc.sbuf_tensor(name, shape, dt))

    def psum(self, name, shape, dt=F32):
        return self.es.enter_context(self.nc.psum_tensor(name, shape, dt))

    def sem(self, name):
        s = self.es.enter_context(self.nc.semaphore(name))
        self.c[name] = 0
        setattr(self, name, s)
        return s

    def inc(self, name, by=1):
        self.c[name] += by
        return self.c[name]

    def close(self):
        self.es.close()
        return self.nc


def alloc_common(p):
    nc = p.nc
    p.idt = p.sbuf("c_idt", [128, 128], BF16)
    p.ones1 = p.sbuf("c_ones1", [1, 128], BF16)
    p.onescol = p.sbuf("c_onescol", [128, 1], BF16)
    p.onesf = p.sbuf("c_onesf", [128, 1], F32)
    p.onesrf = p.sbuf("c_onesrf", [1, 128], F32)
    p.j1 = p.sbuf("c_j1", [128, 1], F32)
    for s in ["sld", "sst", "smm", "sve", "ssc", "str", "sgp"]:
        p.sem(s)


def spacer(nc, p):
    nc.vector.tensor_copy(p.j1[:], p.onesf[:])


def load_consts(p, ident_h, ones1_h, onesf_h, onescol_h, onesrf_h):
    nc = p.nc
    nc.sync.dma_start(out=p.idt[:], in_=ident_h).then_inc(p.sld, 16); p.inc("sld", 16)
    nc.sync.dma_start(out=p.ones1[:], in_=ones1_h).then_inc(p.sld, 16); p.inc("sld", 16)
    nc.sync.dma_start(out=p.onesf[:], in_=onesf_h).then_inc(p.sld, 16); p.inc("sld", 16)
    nc.sync.dma_start(out=p.onescol[:], in_=onescol_h).then_inc(p.sld, 16); p.inc("sld", 16)
    nc.sync.dma_start(out=p.onesrf[:], in_=onesrf_h).then_inc(p.sld, 16); p.inc("sld", 16)


def tern_prep(p, out_writer, w_hbm, nsl, FREE, stage, stage_i, red, rcp, tiny,
              ps_s, ps_r, n_elem):
    nc = p.nc
    CW = 1024
    ncw = (FREE + CW - 1) // CW
    for s in range(nsl):
        for cw in range(ncw):
            w = min(CW, FREE - cw * CW)
            nc.sync.wait_ge(p.sve, p.c["sve"])
            nc.sync.dma_start(out=stage[:, 0:w], in_=w_hbm[s][:, cw * CW:cw * CW + w])\
                .then_inc(p.sld, 16)
            p.inc("sld", 16)
            nc.vector.wait_ge(p.sld, p.c["sld"])
            op = ALU.add
            if cw == 0:
                ins = nc.vector.tensor_reduce(red[:, s:s + 1], stage[:, 0:w],
                                              axis=AX.X, op=op, apply_absolute_value=True)
            else:
                nc.vector.tensor_reduce(tiny["acc1"][:], stage[:, 0:w], axis=AX.X,
                                        op=op, apply_absolute_value=True)
                spacer(nc, p)
                ins = nc.vector.tensor_tensor(red[:, s:s + 1], red[:, s:s + 1],
                                              tiny["acc1"][:], op=ALU.add)
            ins.then_inc(p.sve, 1); p.inc("sve")
    spacer(nc, p)
    ins = nc.vector.tensor_copy(tiny["redb"][:, 0:nsl], red[:, 0:nsl])
    ins.then_inc(p.sve, 1); p.inc("sve")
    nc.tensor.wait_ge(p.sve, p.c["sve"])
    nc.tensor.matmul(ps_s[0:1, 0:nsl], p.onescol[:], tiny["redb"][:, 0:nsl],
                     start=True, stop=True).then_inc(p.smm, 1)
    p.inc("smm")
    nc.vector.wait_ge(p.smm, p.c["smm"])
    nc.vector.tensor_reduce(tiny["t11"][:], ps_s[0:1, 0:nsl], axis=AX.X, op=ALU.add)
    spacer(nc, p)
    nc.vector.tensor_scalar(tiny["t11"][:], tiny["t11"][:], 1.0 / n_elem, 1e-8,
                            op0=ALU.mult, op1=ALU.max)
    spacer(nc, p)
    ins = nc.vector.reciprocal(tiny["t11b"][:], tiny["t11"][:])
    ins.then_inc(p.sve, 1); p.inc("sve")
    nc.tensor.wait_ge(p.sve, p.c["sve"])
    nc.tensor.matmul(ps_r[:, 0:1], p.onesrf[:], tiny["t11b"][:], start=True,
                     stop=True).then_inc(p.smm, 1)
    p.inc("smm")
    nc.vector.wait_ge(p.smm, p.c["smm"])
    nc.vector.tensor_copy(rcp[:], ps_r[:, 0:1])
    ins = nc.vector.tensor_copy(p.j1[:], p.onesf[:])
    ins.then_inc(p.sve, 1); p.inc("sve")
    for s in range(nsl):
        for cw in range(ncw):
            w = min(CW, FREE - cw * CW)
            nc.sync.wait_ge(p.sve, p.c["sve"])
            nc.sync.dma_start(out=stage[:, 0:w], in_=w_hbm[s][:, cw * CW:cw * CW + w])\
                .then_inc(p.sld, 16)
            p.inc("sld", 16)
            nc.vector.wait_ge(p.sld, p.c["sld"])
            nc.vector.tensor_scalar_mul(stage_i[:, 0:w], stage[:, 0:w], rcp[:])
            spacer(nc, p)
            ins = nc.vector.tensor_scalar(out_writer(s)[:, cw * CW:cw * CW + w],
                                          stage_i[:, 0:w], 1.0, -1.0,
                                          op0=ALU.min, op1=ALU.max)
            ins.then_inc(p.sve, 1); p.inc("sve")


def emit_ln_core(p, r_ap, lnbuf, y_out, eps):
    nc = p.nc
    b = lnbuf
    nc.scalar.wait_ge(p.sve, p.c["sve"])
    nc.scalar.activation(b["sq"][:], r_ap, AF.Square, accum_out=b["ssq"][:])
    ins = nc.scalar.activation(b["sq"][:], r_ap, AF.Identity, accum_out=b["s1"][:])
    ins.then_inc(p.ssc, 1); p.inc("ssc")
    nc.vector.wait_ge(p.ssc, p.c["ssc"])
    nc.vector.tensor_scalar_mul(b["mneg"][:], b["s1"][:], -1.0 / 1024.0)
    nc.vector.tensor_scalar(b["var0"][:], b["ssq"][:], 1.0 / 1024.0, eps,
                            op0=ALU.mult, op1=ALU.add)
    nc.vector.tensor_tensor(b["msq"][:], b["mneg"][:], b["mneg"][:], op=ALU.mult)
    spacer(nc, p)
    ins = nc.vector.scalar_tensor_tensor(b["var"][:], b["msq"][:], -1.0, b["var0"][:],
                                         op0=ALU.mult, op1=ALU.add)
    ins.then_inc(p.sve, 1); p.inc("sve")
    nc.scalar.wait_ge(p.sve, p.c["sve"])
    ins = nc.scalar.activation(b["rstd0"][:], b["var"][:], AF.Sqrt)
    ins.then_inc(p.ssc, 1); p.inc("ssc")
    nc.vector.wait_ge(p.ssc, p.c["ssc"])
    nc.vector.reciprocal(b["rstd"][:], b["rstd0"][:])
    spacer(nc, p)
    ins = nc.vector.tensor_scalar(y_out, r_ap, b["mneg"][:], b["rstd"][:],
                                  op0=ALU.add, op1=ALU.mult)
    ins.then_inc(p.sve, 1); p.inc("sve")


def emit_ln(p, ps0, ps1, xres, lnbuf, y_out, eps):
    nc = p.nc
    b = lnbuf
    nc.vector.tensor_tensor(b["r"][:, 0:512], ps0, xres[:, 0:512], op=ALU.add)
    ins = nc.vector.tensor_tensor(b["r"][:, 512:1024], ps1, xres[:, 512:1024], op=ALU.add)
    ins.then_inc(p.sve, 1); p.inc("sve")
    emit_ln_core(p, b["r"][:], lnbuf, y_out, eps)


def alloc_lnbuf(p, tag=""):
    return {
        "r": p.sbuf(f"ln{tag}_r", [128, 1024], F32),
        "sq": p.sbuf(f"ln{tag}_sq", [128, 1024], BF16),
        "s1": p.sbuf(f"ln{tag}_s1", [128, 1], F32),
        "ssq": p.sbuf(f"ln{tag}_ssq", [128, 1], F32),
        "mneg": p.sbuf(f"ln{tag}_mneg", [128, 1], F32),
        "msq": p.sbuf(f"ln{tag}_msq", [128, 1], F32),
        "var0": p.sbuf(f"ln{tag}_var0", [128, 1], F32),
        "var": p.sbuf(f"ln{tag}_var", [128, 1], F32),
        "rstd0": p.sbuf(f"ln{tag}_rstd0", [128, 1], F32),
        "rstd": p.sbuf(f"ln{tag}_rstd", [128, 1], F32),
    }


def alloc_tiny(p, tag=""):
    return {
        "t11": p.sbuf(f"tn{tag}_t11", [1, 1], F32),
        "t11b": p.sbuf(f"tn{tag}_t11b", [1, 1], F32),
        "t11c": p.sbuf(f"tn{tag}_t11c", [1, 1], BF16),
        "redb": p.sbuf(f"tn{tag}_redb", [128, 32], BF16),
        "acc1": p.sbuf(f"tn{tag}_acc1", [128, 1], F32),
    }


def build_l0ffn(gelu_fn=AF.Gelu):
    p = P()
    nc = p.nc
    FF = 4 * D
    NFS = FF // 128
    CH = 512
    NCH = T // CH
    NTC = CH // 128

    x_in = nc.dram_tensor("x_in", [T, D], F32, kind="ExternalInput").ap()
    w1t = nc.dram_tensor("w1t", [8, 128, FF], F32, kind="ExternalInput").ap()
    b1a = nc.dram_tensor("b1a", [128, NFS], F32, kind="ExternalInput").ap()
    w2t = nc.dram_tensor("w2t", [32, 128, D], F32, kind="ExternalInput").ap()
    b2r = nc.dram_tensor("b2r", [1, D], F32, kind="ExternalInput").ap()
    ident_h = nc.dram_tensor("ident", [128, 128], BF16, kind="ExternalInput").ap()
    ones1_h = nc.dram_tensor("ones1", [1, 128], BF16, kind="ExternalInput").ap()
    onesf_h = nc.dram_tensor("onesf", [128, 1], F32, kind="ExternalInput").ap()
    onescol_h = nc.dram_tensor("onescol", [128, 1], BF16, kind="ExternalInput").ap()
    onesrf_h = nc.dram_tensor("onesrf", [1, 128], F32, kind="ExternalInput").ap()
    x_out = nc.dram_tensor("x_out", [T, D], F32, kind="ExternalOutput").ap()
    dbg_w1 = nc.dram_tensor("dbg_w1", [128, 8, 4096], BF16, kind="ExternalOutput").ap()
    dbg_rcp = nc.dram_tensor("dbg_rcp", [2, 128, 1], F32, kind="ExternalOutput").ap()
    dbg_hT = nc.dram_tensor("dbg_hT", [128, 32, 512], BF16, kind="ExternalOutput").ap()
    dbg_xT = nc.dram_tensor("dbg_xT", [128, 8, 512], BF16, kind="ExternalOutput").ap()

    alloc_common(p)

    w1b = p.sbuf("w1b", [128, 8, FF], BF16)
    w2b = p.sbuf("w2b", [128, 32, D], BF16)
    stage = p.sbuf("stage", [128, 1024], F32)
    stage_i = p.sbuf("stage_i", [128, 1024], I32)
    red = p.sbuf("red", [128, 32], F32)
    rcp1 = p.sbuf("rcp1", [128, 1], F32)
    rcp2 = p.sbuf("rcp2", [128, 1], F32)
    b1t = p.sbuf("b1t", [128, NFS], F32)
    b2b = p.sbuf("b2b", [1, D], BF16)
    b2f = p.sbuf("b2f", [1, D], F32)
    tiny = alloc_tiny(p)
    lnbuf = alloc_lnbuf(p)
    xc = p.sbuf("xc", [128, NTC, D], F32)
    xcb = p.sbuf("xcb", [128, D], BF16)
    xT = p.sbuf("xT", [128, 8, CH], BF16)
    hT = p.sbuf("hT", [128, NFS, CH], BF16)

    ps_h = [p.psum(f"ps_h{i}", [128, 512], F32) for i in range(2)]
    ps_y = [p.psum(f"ps_y{i}", [128, 512], F32) for i in range(2)]
    ps_t = [p.psum(f"ps_t{i}", [128, 128], BF16) for i in range(2)]
    ps_s = p.psum("ps_s", [1, 32], F32)
    ps_r = p.psum("ps_r", [128, 1], F32)

    blk = p.es.enter_context(nc.Block())

    load_consts(p, ident_h, ones1_h, onesf_h, onescol_h, onesrf_h)
    nc.sync.dma_start(out=b1t[:], in_=b1a).then_inc(p.sld, 16); p.inc("sld", 16)
    nc.sync.dma_start(out=b2f[:], in_=b2r).then_inc(p.sld, 16); p.inc("sld", 16)
    nc.scalar.wait_ge(p.sld, p.c["sld"])
    ins = nc.scalar.activation(b2b[:], b2f[:], AF.Copy)
    ins.then_inc(p.ssc, 1); p.inc("ssc")

    tern_prep(p, lambda s: w1b[:, s, :], w1t, 8, FF, stage, stage_i, red, rcp1,
              tiny, ps_s, ps_r, FF * D)
    tern_prep(p, lambda s: w2b[:, s, :], w2t, 32, D, stage, stage_i, red, rcp2,
              tiny, ps_s, ps_r, FF * D)
    w_ready_sve = p.c["sve"]
    b2_ready_ssc = 1

    ps_h_free = [0, 0]
    ps_y_free = 0
    ps_t_free = [0, 0]

    for ch in range(NCH):
        if ch > 0:
            nc.sync.wait_ge(p.sst, p.c["sst"])
        src = x_in[ch * CH:(ch + 1) * CH, :].rearrange("(tl q) d -> q tl d", q=128)
        nc.sync.dma_start(out=xc[:], in_=src).then_inc(p.sld, 16)
        p.inc("sld", 16)
        x_loaded = p.c["sld"]
        nc.scalar.wait_ge(p.sld, x_loaded)
        k = 0
        for tl in range(NTC):
            ins = nc.scalar.activation(xcb[:], xc[:, tl, :], AF.Copy)
            ins.then_inc(p.ssc, 1); p.inc("ssc")
            nc.tensor.wait_ge(p.ssc, p.c["ssc"])
            for ds in range(8):
                pt = ps_t[k % 2]
                if ps_t_free[k % 2] > 0:
                    nc.tensor.wait_ge(p.ssc, ps_t_free[k % 2])
                ins = nc.tensor.transpose(pt[:], xcb[:, ds * 128:(ds + 1) * 128],
                                          p.idt[:])
                ins.then_inc(p.smm, 1); p.inc("smm")
                nc.scalar.wait_ge(p.smm, p.c["smm"])
                ins = nc.scalar.activation(xT[:, ds, tl * 128:(tl + 1) * 128], pt[:],
                                           AF.Copy)
                ins.then_inc(p.ssc, 1); p.inc("ssc")
                ps_t_free[k % 2] = p.c["ssc"]
                k += 1
        xT_done = p.c["ssc"]
        nc.tensor.wait_ge(p.ssc, xT_done)
        if ch == 0:
            nc.tensor.wait_ge(p.sve, w_ready_sve)
        for fs in range(NFS):
            for tc2 in range(CH // 512):
                pid = fs % 2
                if ps_h_free[pid] > 0:
                    nc.tensor.wait_ge(p.ssc, ps_h_free[pid])
                for s in range(8):
                    ins = nc.tensor.matmul(
                        ps_h[pid][:], w1b[:, s, fs * 128:(fs + 1) * 128],
                        xT[:, s, tc2 * 512:(tc2 + 1) * 512],
                        start=(s == 0), stop=(s == 7))
                ins.then_inc(p.smm, 1); p.inc("smm")
                nc.scalar.wait_ge(p.smm, p.c["smm"])
                ins = nc.scalar.activation(hT[:, fs, tc2 * 512:(tc2 + 1) * 512],
                                           ps_h[pid][:], gelu_fn,
                                           bias=b1t[:, fs:fs + 1], scale=1.0)
                ins.then_inc(p.ssc, 1); p.inc("ssc")
                ps_h_free[pid] = p.c["ssc"]
        hT_done = p.c["ssc"]
        nc.tensor.wait_ge(p.ssc, hT_done)
        for tl in range(NTC):
            if ps_y_free > 0:
                nc.tensor.wait_ge(p.sve, ps_y_free)
            for dc in range(2):
                for fs in range(NFS):
                    nc.tensor.matmul(ps_y[dc][:], hT[:, fs, tl * 128:(tl + 1) * 128],
                                     w2b[:, fs, dc * 512:(dc + 1) * 512],
                                     start=(fs == 0), stop=False)
                ins = nc.tensor.matmul(ps_y[dc][:], p.ones1[:],
                                       b2b[0:1, dc * 512:(dc + 1) * 512],
                                       start=False, stop=True)
            ins.then_inc(p.smm, 1); p.inc("smm")
            nc.vector.wait_ge(p.smm, p.c["smm"])
            emit_ln(p, ps_y[0][:], ps_y[1][:], xc[:, tl, :], lnbuf,
                    xc[:, tl, :], 1e-5)
            ps_y_free = p.c["sve"] - 2
        ln_done = p.c["sve"]
        nc.gpsimd.wait_ge(p.sve, ln_done)
        dst = x_out[ch * CH:(ch + 1) * CH, :].rearrange("(tl q) d -> q tl d", q=128)
        nc.gpsimd.dma_start(out=dst, in_=xc[:]).then_inc(p.sst, 16)
        p.inc("sst", 16)

    nc.gpsimd.dma_start(out=dbg_w1, in_=w1b[:]).then_inc(p.sst, 16); p.inc("sst", 16)
    nc.gpsimd.dma_start(out=dbg_rcp[0], in_=rcp1[:]).then_inc(p.sst, 16); p.inc("sst", 16)
    nc.gpsimd.dma_start(out=dbg_rcp[1], in_=rcp2[:]).then_inc(p.sst, 16); p.inc("sst", 16)
    nc.gpsimd.dma_start(out=dbg_hT, in_=hT[:]).then_inc(p.sst, 16); p.inc("sst", 16)
    nc.gpsimd.dma_start(out=dbg_xT, in_=xT[:]).then_inc(p.sst, 16); p.inc("sst", 16)
    nc.sync.wait_ge(p.sst, p.c["sst"])
    return p


def build_l1ffn():
    p = P()
    nc = p.nc
    FH = 3072
    NFS = FH // 128
    CH = 512
    NCH = T // CH
    NTC = CH // 128

    x_in = nc.dram_tensor("x_in", [T, D], F32, kind="ExternalInput").ap()
    wgT = nc.dram_tensor("wgT", [8, 128, FH], I32, kind="ExternalInput").ap()
    wuT = nc.dram_tensor("wuT", [8, 128, FH], I32, kind="ExternalInput").ap()
    wdT = nc.dram_tensor("wdT", [24, 128, D], I32, kind="ExternalInput").ap()
    ident_h = nc.dram_tensor("ident", [128, 128], BF16, kind="ExternalInput").ap()
    ones1_h = nc.dram_tensor("ones1", [1, 128], BF16, kind="ExternalInput").ap()
    onesf_h = nc.dram_tensor("onesf", [128, 1], F32, kind="ExternalInput").ap()
    onescol_h = nc.dram_tensor("onescol", [128, 1], BF16, kind="ExternalInput").ap()
    onesrf_h = nc.dram_tensor("onesrf", [1, 128], F32, kind="ExternalInput").ap()
    x_out = nc.dram_tensor("x_out", [T, D], F32, kind="ExternalOutput").ap()

    alloc_common(p)

    wgB = p.sbuf("wgB", [128, 8, FH], BF16)
    wuB = p.sbuf("wuB", [128, 8, FH], BF16)
    wdB = p.sbuf("wdB", [128, 24, D], BF16)
    stage = p.sbuf("stage", [128, 1024], I32)
    lnbuf = alloc_lnbuf(p)
    xc = p.sbuf("xc", [128, NTC, D], F32)
    xcb = p.sbuf("xcb", [128, D], BF16)
    xT = p.sbuf("xT", [128, 8, CH], BF16)
    hT = p.sbuf("hT", [128, NFS, CH], BF16)
    gT = p.sbuf("gT", [128, CH], BF16)
    uT = p.sbuf("uT", [128, CH], BF16)

    ps_g = p.psum("ps_g", [128, 512], F32)
    ps_u = p.psum("ps_u", [128, 512], F32)
    ps_y = [p.psum(f"ps_y{i}", [128, 512], F32) for i in range(2)]
    ps_t = [p.psum(f"ps_t{i}", [128, 128], BF16) for i in range(2)]

    blk = p.es.enter_context(nc.Block())
    load_consts(p, ident_h, ones1_h, onesf_h, onescol_h, onesrf_h)

    def dequant(dst3, src3, nsl, FREE):
        CW = 1024
        ncw = FREE // CW
        for s in range(nsl):
            for cw in range(ncw):
                nc.sync.wait_ge(p.sve, p.c["sve"])
                nc.sync.dma_start(out=stage[:], in_=src3[s][:, cw * CW:(cw + 1) * CW])\
                    .then_inc(p.sld, 16)
                p.inc("sld", 16)
                nc.vector.wait_ge(p.sld, p.c["sld"])
                ins = nc.vector.tensor_scalar_mul(dst3[:, s, cw * CW:(cw + 1) * CW],
                                                  stage[:], 1.0 / 127.0)
                ins.then_inc(p.sve, 1); p.inc("sve")

    dequant(wgB, wgT, 8, FH)
    dequant(wuB, wuT, 8, FH)
    dequant(wdB, wdT, 24, D)
    w_ready_sve = p.c["sve"]

    ps_g_free = 0
    ps_u_free = 0
    ps_y_free = 0
    ps_t_free = [0, 0]

    for ch in range(NCH):
        if ch > 0:
            nc.sync.wait_ge(p.sst, p.c["sst"])
        src = x_in[ch * CH:(ch + 1) * CH, :].rearrange("(tl q) d -> q tl d", q=128)
        nc.sync.dma_start(out=xc[:], in_=src).then_inc(p.sld, 16)
        p.inc("sld", 16)
        x_loaded = p.c["sld"]
        nc.scalar.wait_ge(p.sld, x_loaded)
        k = 0
        for tl in range(NTC):
            ins = nc.scalar.activation(xcb[:], xc[:, tl, :], AF.Copy)
            ins.then_inc(p.ssc, 1); p.inc("ssc")
            nc.tensor.wait_ge(p.ssc, p.c["ssc"])
            for ds in range(8):
                pt = ps_t[k % 2]
                if ps_t_free[k % 2] > 0:
                    nc.tensor.wait_ge(p.ssc, ps_t_free[k % 2])
                ins = nc.tensor.transpose(pt[:], xcb[:, ds * 128:(ds + 1) * 128],
                                          p.idt[:])
                ins.then_inc(p.smm, 1); p.inc("smm")
                nc.scalar.wait_ge(p.smm, p.c["smm"])
                ins = nc.scalar.activation(xT[:, ds, tl * 128:(tl + 1) * 128], pt[:],
                                           AF.Copy)
                ins.then_inc(p.ssc, 1); p.inc("ssc")
                ps_t_free[k % 2] = p.c["ssc"]
                k += 1
        xT_done = p.c["ssc"]
        nc.tensor.wait_ge(p.ssc, xT_done)
        if ch == 0:
            nc.tensor.wait_ge(p.sve, w_ready_sve)
        for fs in range(NFS):
            if ps_g_free > 0:
                nc.tensor.wait_ge(p.sve, ps_g_free)
            for s in range(8):
                nc.tensor.matmul(ps_g[:], wgB[:, s, fs * 128:(fs + 1) * 128],
                                 xT[:, s, :], start=(s == 0), stop=(s == 7))
            for s in range(8):
                ins = nc.tensor.matmul(ps_u[:], wuB[:, s, fs * 128:(fs + 1) * 128],
                                       xT[:, s, :], start=(s == 0), stop=(s == 7))
            ins.then_inc(p.smm, 1); p.inc("smm")
            nc.scalar.wait_ge(p.smm, p.c["smm"])
            nc.scalar.activation(gT[:], ps_g[:], AF.Silu)
            ins = nc.scalar.activation(uT[:], ps_u[:], AF.Copy)
            ins.then_inc(p.ssc, 1); p.inc("ssc")
            nc.vector.wait_ge(p.ssc, p.c["ssc"])
            ins = nc.vector.tensor_tensor(hT[:, fs, :], gT[:], uT[:], op=ALU.mult)
            ins.then_inc(p.sve, 1); p.inc("sve")
            ps_g_free = p.c["sve"]
        hT_done = p.c["sve"]
        nc.tensor.wait_ge(p.sve, hT_done)
        for tl in range(NTC):
            if ps_y_free > 0:
                nc.tensor.wait_ge(p.sve, ps_y_free)
            for dc in range(2):
                for fs in range(NFS):
                    ins = nc.tensor.matmul(ps_y[dc][:],
                                           hT[:, fs, tl * 128:(tl + 1) * 128],
                                           wdB[:, fs, dc * 512:(dc + 1) * 512],
                                           start=(fs == 0), stop=(fs == NFS - 1))
            ins.then_inc(p.smm, 1); p.inc("smm")
            nc.vector.wait_ge(p.smm, p.c["smm"])
            emit_ln(p, ps_y[0][:], ps_y[1][:], xc[:, tl, :], lnbuf,
                    xc[:, tl, :], 1e-5)
            ps_y_free = p.c["sve"] - 2
        ln_done = p.c["sve"]
        nc.gpsimd.wait_ge(p.sve, ln_done)
        dst = x_out[ch * CH:(ch + 1) * CH, :].rearrange("(tl q) d -> q tl d", q=128)
        nc.gpsimd.dma_start(out=dst, in_=xc[:]).then_inc(p.sst, 16)
        p.inc("sst", 16)

    nc.sync.wait_ge(p.sst, p.c["sst"])
    return p



DBG_TILE = 0

def build_l0attn(dbg=False):
    p = P()
    nc = p.nc
    x_in = nc.dram_tensor("x_in", [T, D], F32, kind="ExternalInput").ap()
    tri_h = nc.dram_tensor("tri", [128, 128], BF16, kind="ExternalInput").ap()
    wpe_h = nc.dram_tensor("wpe", [128, 124], F32, kind="ExternalInput").ap()
    bpe_h = nc.dram_tensor("bpe", [1, 1024], F32, kind="ExternalInput").ap()
    stt_h = nc.dram_tensor("stt", [1, 496], F32, kind="ExternalInput").ap()
    recip_h = nc.dram_tensor("recip", [128, 32], F32, kind="ExternalInput").ap()
    mask0_h = nc.dram_tensor("mask0", [128, 496], F32, kind="ExternalInput").ap()
    wfcT_h = nc.dram_tensor("wfcT", [8, 128, D], F32, kind="ExternalInput").ap()
    gam_h = nc.dram_tensor("gam", [1, D], F32, kind="ExternalInput").ap()
    bet_h = nc.dram_tensor("bet", [1, D], F32, kind="ExternalInput").ap()
    ident_h = nc.dram_tensor("ident", [128, 128], BF16, kind="ExternalInput").ap()
    ones1_h = nc.dram_tensor("ones1", [1, 128], BF16, kind="ExternalInput").ap()
    onesf_h = nc.dram_tensor("onesf", [128, 1], F32, kind="ExternalInput").ap()
    onescol_h = nc.dram_tensor("onescol", [128, 1], BF16, kind="ExternalInput").ap()
    onesrf_h = nc.dram_tensor("onesrf", [1, 128], F32, kind="ExternalInput").ap()
    ibig_h = nc.dram_tensor("ibig", [128, 260], BF16, kind="ExternalInput").ap()
    jbig_h = nc.dram_tensor("jbig", [30, 286], BF16, kind="ExternalInput").ap()
    x_out = nc.dram_tensor("x_out", [T, D], F32, kind="ExternalOutput").ap()
    if dbg:
        dbg_ci = nc.dram_tensor("dbg_ci", [128, D], F32, kind="ExternalOutput").ap()
        dbg_en = nc.dram_tensor("dbg_en", [128, 496], F32, kind="ExternalOutput").ap()
        dbg_w = nc.dram_tensor("dbg_w", [128, 496], F32, kind="ExternalOutput").ap()
        dbg_cv = nc.dram_tensor("dbg_cv", [128, D], F32, kind="ExternalOutput").ap()
        dbg_vr = nc.dram_tensor("dbg_vr", [128, 1024], BF16, kind="ExternalOutput").ap()
        dbg_wfcb = nc.dram_tensor("dbg_wfcb", [128, 8192], BF16, kind="ExternalOutput").ap()
        dbg_xh = nc.dram_tensor("dbg_xh", [128, D], F32, kind="ExternalOutput").ap()

    alloc_common(p)
    p.sem("stl")

    lnbuf = alloc_lnbuf(p)
    xall = p.sbuf("xall", [128, NT, D], BF16)
    ci_cur = p.sbuf("ci_cur", [128, D], BF16)
    ci_tail = p.sbuf("ci_tail", [30, D], BF16)
    ciT = p.sbuf("ciT", [128, 8, 128], BF16)
    carry = p.sbuf("carry", [1, D], BF16)
    wpeB = p.sbuf("wpeB", [128, 124], BF16)
    bpeB = p.sbuf("bpeB", [1, 1024], BF16)
    gamr = p.sbuf("gamr", [1, D], F32)
    betr = p.sbuf("betr", [1, D], F32)
    bper = p.sbuf("bper", [1, 1024], F32)
    sttr = p.sbuf("sttr", [1, 496], F32)
    sttb = p.sbuf("sttb", [128, 496], F32)
    maskb = p.sbuf("maskb", [128, 496], F32)
    recipb = p.sbuf("recipb", [128, 32], F32)
    gamb = p.sbuf("gamb", [128, D], F32)
    betb = p.sbuf("betb", [128, D], F32)
    wfcB = p.sbuf("wfcB", [128, 8, D], BF16)
    sg = p.sbuf("sg", [128, 496], F32)
    en = p.sbuf("en", [128, 496], F32)
    enb = p.sbuf("enb", [128, 496], F32)
    mx = p.sbuf("mx", [128, 16], F32)
    ssum = p.sbuf("ssum", [128, 16], F32)
    rs = p.sbuf("rs", [128, 16], F32)
    w_bf = p.sbuf("w_bf", [128, 496], BF16)
    WS = p.sbuf("WS", [128, 31, 16], BF16)
    WSc = p.sbuf("WSc", [30, 31, 16], BF16)
    rhsring = p.sbuf("rhsring", [128, 4, D], BF16)
    rhsB = p.sbuf("rhsB", [30, 2, D], BF16)
    ibig = p.sbuf("ibigs", [128, 260], BF16)
    jbig = p.sbuf("jbigs", [30, 286], BF16)
    cv_bf = p.sbuf("cv_bf", [128, D], BF16)
    dbg_tmp = p.sbuf("dbg_tmp", [128, D], F32) if True else None
    trib = p.sbuf("trib", [128, 128], BF16)
    xhat = p.sbuf("xhat", [128, D], F32)
    bx = p.sbuf("bx", [128, D], F32)
    r2 = xhat
    yout = bx

    ps_big = [p.psum(f"ps_big{i}", [128, 512], F32) for i in range(2)]
    ps_t = [p.psum(f"ps_t{i}", [128, 128], BF16) for i in range(2)]
    ps_c = p.psum("ps_c", [1, 1024], F32)

    stage = lnbuf["r"]
    blk = p.es.enter_context(nc.Block())
    load_consts(p, ident_h, ones1_h, onesf_h, onescol_h, onesrf_h)
    for dst, srch in [(recipb[:], recip_h), (maskb[:], mask0_h), (gamr[:], gam_h),
                      (betr[:], bet_h), (bper[:], bpe_h), (sttr[:], stt_h),
                      (trib[:], tri_h), (stage[:, 0:124], wpe_h),
                      (ibig[:], ibig_h), (jbig[:], jbig_h)]:
        nc.sync.dma_start(out=dst, in_=srch).then_inc(p.sld, 16)
        p.inc("sld", 16)
    x_load_pending = True
    all_loaded = None
    nc.scalar.wait_ge(p.sld, p.c["sld"])
    nc.scalar.activation(wpeB[:], stage[:, 0:124], AF.Copy)
    ins = nc.scalar.activation(bpeB[:], bper[:], AF.Copy)
    ins.then_inc(p.ssc, 1); p.inc("ssc")
    def bc_full(dst, row):
        nc.tensor.wait_ge(p.sld, p.c["sld"])
        if p.c["sve"] > 0:
            nc.tensor.wait_ge(p.sve, p.c["sve"])
        n = dst.shape[1]
        mmi = nc.tensor.matmul(ps_big[0][:, 0:min(512, n)], p.onesrf[:],
                               row[0:1, 0:min(512, n)], start=True, stop=True)
        if n > 512:
            mmi = nc.tensor.matmul(ps_big[1][:, 0:n - 512], p.onesrf[:],
                                   row[0:1, 512:n], start=True, stop=True)
        mmi.then_inc(p.smm, 1); p.inc("smm")
        nc.vector.wait_ge(p.smm, p.c["smm"])
        nc.vector.tensor_copy(dst[:, 0:min(512, n)], ps_big[0][:, 0:min(512, n)])
        if n > 512:
            ins = nc.vector.tensor_copy(dst[:, 512:n], ps_big[1][:, 0:n - 512])
        else:
            ins = nc.vector.tensor_copy(p.j1[:], p.onesf[:])
        ins.then_inc(p.sve, 1); p.inc("sve")
    bc_full(sttb, sttr)
    bc_full(gamb, gamr)
    bc_full(betb, betr)
    for s in range(8):
        nc.sync.wait_ge(p.sve, p.c["sve"])
        nc.sync.dma_start(out=stage[:], in_=wfcT_h[s]).then_inc(p.sld, 16)
        p.inc("sld", 16)
        nc.vector.wait_ge(p.sld, p.c["sld"])
        ins = nc.vector.tensor_copy(wfcB[:, s, :], stage[:])
        ins.then_inc(p.sve, 1); p.inc("sve")
    nc.vector.memset(WS[:], 0.0)
    nc.vector.memset(WSc[:], 0.0)
    nc.vector.memset(rhsring[:], 0.0)
    ins = nc.vector.memset(rhsB[:], 0.0)
    ins.then_inc(p.sve, 1); p.inc("sve")
    nc.gpsimd.wait_ge(p.sve, p.c["sve"])
    nc.sync.wait_ge(p.sve, p.c["sve"])
    for tl in range(NT):
        nc.sync.wait_ge(p.ssc, p.c["ssc"])
        nc.sync.dma_start(out=stage[:], in_=x_in[tl * 128:(tl + 1) * 128, :])\
            .then_inc(p.sld, 16)
        p.inc("sld", 16)
        nc.scalar.wait_ge(p.sld, p.c["sld"])
        ins = nc.scalar.activation(xall[:, tl, :], stage[:], AF.Copy)
        ins.then_inc(p.ssc, 1); p.inc("ssc")
    all_loaded = p.c["ssc"]

    ps_t_free = [0, 0]

    for i in range(NT):
        xtb = xall[:, i, :]
        if i == 0:
            nc.tensor.wait_ge(p.ssc, all_loaded)
            nc.tensor.wait_ge(p.sve, p.c["sve"])
        else:
            nc.tensor.wait_ge(p.sve, p.c["sve"])
        for dc in range(2):
            nc.tensor.matmul(ps_big[dc][:], trib[:], xtb[:, dc * 512:(dc + 1) * 512],
                             start=True, stop=(i == 0))
            if i > 0:
                nc.tensor.matmul(ps_big[dc][:], p.ones1[:],
                                 carry[0:1, dc * 512:(dc + 1) * 512],
                                 start=False, stop=True)
        for dc in range(2):
            ins = nc.tensor.matmul(ps_c[0:1, dc * 512:(dc + 1) * 512], p.onescol[:],
                                   xtb[:, dc * 512:(dc + 1) * 512],
                                   start=(i == 0), stop=(i == NT - 1))
        ins.then_inc(p.smm, 1); p.inc("smm")
        nc.scalar.wait_ge(p.smm, p.c["smm"])
        nc.scalar.activation(ci_cur[:, 0:512], ps_big[0][:], AF.Identity,
                             scale=recipb[:, i:i + 1], bias=0.0)
        nc.scalar.activation(ci_cur[:, 512:1024], ps_big[1][:], AF.Identity,
                             scale=recipb[:, i:i + 1], bias=0.0)
        ins = nc.scalar.activation(carry[:], ps_c[0:1, :], AF.Copy)
        ins.then_inc(p.ssc, 1); p.inc("ssc")
        ci_ssc = p.c["ssc"]
        nc.tensor.wait_ge(p.ssc, ci_ssc)
        for ds in range(8):
            pt = ps_t[ds % 2]
            if ps_t_free[ds % 2] > 0:
                nc.tensor.wait_ge(p.ssc, ps_t_free[ds % 2])
            ins = nc.tensor.transpose(pt[:], ci_cur[:, ds * 128:(ds + 1) * 128], p.idt[:])
            ins.then_inc(p.smm, 1); p.inc("smm")
            nc.scalar.wait_ge(p.smm, p.c["smm"])
            ins = nc.scalar.activation(ciT[:, ds, :], pt[:], AF.Copy)
            ins.then_inc(p.ssc, 1); p.inc("ssc")
            ps_t_free[ds % 2] = p.c["ssc"]
        nc.tensor.wait_ge(p.ssc, p.c["ssc"])
        for s in range(8):
            pb = ps_big[s // 4]
            c0 = (s % 4) * 128
            nc.tensor.matmul(pb[:, c0:c0 + 124], ciT[:, s, :], wpeB[:],
                             start=True, stop=False)
            ins = nc.tensor.matmul(pb[:, c0:c0 + 124], p.ones1[:],
                                   bpeB[0:1, s * 128:s * 128 + 124],
                                   start=False, stop=True)
        ins.then_inc(p.smm, 1); p.inc("smm")
        nc.scalar.wait_ge(p.smm, p.c["smm"])
        sg3a = sg[:].rearrange("p (k h) -> p h k", h=16)
        for half in range(2):
            pb3 = ps_big[half][:].rearrange("p (s r) -> p s r", r=128)
            for hp in range(2):
                g0 = 8 * half + hp
                ins = nc.scalar.activation(
                    sg3a[:, g0:min(g0 + 8, 16):2, :].rearrange("p b k -> p k b")
                    if False else sg3a[:, g0:min(g0 + 8, 16):2, :],
                    pb3[:, :, 31 + hp * 62:62 + hp * 62],
                    AF.Sigmoid)
        ins.then_inc(p.ssc, 1); p.inc("ssc")
        nc.vector.wait_ge(p.ssc, p.c["ssc"])
        en3 = en[:].rearrange("p (k h) -> p h k", h=16)
        for half in range(2):
            pb3 = ps_big[half][:].rearrange("p (s r) -> p s r", r=128)
            for hp in range(2):
                g0 = 8 * half + hp
                nc.vector.tensor_tensor(
                    en3[:, g0:min(g0 + 8, 16):2, :],
                    sg3a[:, g0:min(g0 + 8, 16):2, :],
                    pb3[:, :, hp * 62:hp * 62 + 31], op=ALU.mult)
        en_sve_raw = p.c["sve"]
        spacer(nc, p)
        ins = nc.vector.tensor_tensor(enb[:], en[:], sttb[:], op=ALU.add)
        ins.then_inc(p.sve, 1); p.inc("sve")
        psbig_free_sve = p.c["sve"]
        enb_hk = enb[:].rearrange("p (k h) -> p h k", h=16)
        en_hk = en[:].rearrange("p (k h) -> p h k", h=16)
        nc.vector.tensor_reduce(mx[:].unsqueeze(-1), enb_hk, axis=AX.X, op=ALU.max)
        spacer(nc, p)
        mxb = mx[:].unsqueeze(-1).broadcast_to([128, 16, 31])
        ins = nc.vector.tensor_tensor(en_hk, enb_hk, mxb, op=ALU.subtract)
        ins.then_inc(p.sve, 1); p.inc("sve")
        nc.scalar.wait_ge(p.sve, p.c["sve"])
        ins = nc.scalar.activation(enb[:], en[:], AF.Exp)
        ins.then_inc(p.ssc, 1); p.inc("ssc")
        nc.vector.wait_ge(p.ssc, p.c["ssc"])
        nc.vector.tensor_reduce(ssum[:].unsqueeze(-1), enb_hk, axis=AX.X, op=ALU.add)
        spacer(nc, p)
        nc.vector.reciprocal(rs[:], ssum[:])
        spacer(nc, p)
        rsb = rs[:].unsqueeze(-1).broadcast_to([128, 16, 31])
        nc.vector.tensor_tensor(en_hk, enb_hk, rsb, op=ALU.mult)
        if i == 0:
            spacer(nc, p)
            nc.vector.tensor_tensor(en[:], en[:], maskb[:], op=ALU.mult)
        else:
            spacer(nc, p)
        ins = nc.vector.tensor_copy(w_bf[:], en[:])
        ins.then_inc(p.sve, 1); p.inc("sve")
        wbf_sve = p.c["sve"]
        if dbg and i == DBG_TILE:
            nc.gpsimd.wait_ge(p.sve, wbf_sve)
            nc.gpsimd.dma_start(out=dbg_ci, in_=ci_cur[:]).then_inc(p.sst, 16); p.inc("sst", 16)
            nc.gpsimd.dma_start(out=dbg_en, in_=enb[:]).then_inc(p.sst, 16); p.inc("sst", 16)
            nc.gpsimd.dma_start(out=dbg_w, in_=en[:]).then_inc(p.sst, 16); p.inc("sst", 16)
        nc.gpsimd.wait_ge(p.sve, wbf_sve)
        for d_ in range(31):
            nc.gpsimd.dma_start(out=WS[0:128 - d_, d_, :],
                                in_=w_bf[d_:128, (30 - d_) * 16:(31 - d_) * 16])\
                .then_inc(p.stl, 16)
            p.inc("stl", 16)
        if i > 0:
            for d_ in range(1, 31):
                nc.gpsimd.dma_start(out=WSc[30 - d_:30, d_, :],
                                    in_=w_bf[0:d_, (30 - d_) * 16:(31 - d_) * 16])\
                    .then_inc(p.stl, 16)
                p.inc("stl", 16)
        ws_done = p.c["stl"]
        nc.vector.wait_ge(p.stl, ws_done)
        nc.vector.wait_ge(p.ssc, p.c["ssc"])
        ci_hd = ci_cur[:].rearrange("p (h e) -> p h e", h=16)
        first_mm = True
        ring_free = [0, 0, 0, 0]
        for d_ in range(31):
            rb = rhsring[:, d_ % 4, :]
            if ring_free[d_ % 4] > 0:
                nc.vector.wait_ge(p.smm, ring_free[d_ % 4])
            wsb = WS[:, d_, :].unsqueeze(-1).broadcast_to([128, 16, 64])
            ins = nc.vector.tensor_tensor(rb.rearrange("p (h e) -> p h e", h=16),
                                          ci_hd, wsb, op=ALU.mult)
            ins.then_inc(p.sve, 1); p.inc("sve")
            nc.tensor.wait_ge(p.sve, p.c["sve"])
            lhs = ibig[:, 128 - d_:256 - d_]
            for dc in range(2):
                ins = nc.tensor.matmul(ps_big[dc][:], lhs,
                                       rb[dc * 512:(dc + 1) * 512]
                                       if False else rb.rearrange("p f -> p f")[:, dc * 512:(dc + 1) * 512],
                                       start=first_mm, stop=False)
            first_mm = False
            ins.then_inc(p.smm, 1); p.inc("smm")
            ring_free[d_ % 4] = p.c["smm"]
        if i > 0:
            ci_t_hd = ci_tail[:].rearrange("p (h e) -> p h e", h=16)
            for d_ in range(1, 31):
                rb = rhsB[:, d_ % 2, :]
                if d_ > 2:
                    nc.vector.wait_ge(p.smm, p.c["smm"] - 1)
                wsb = WSc[:, d_, :].unsqueeze(-1).broadcast_to([30, 16, 64])
                ins = nc.vector.tensor_tensor(rb.rearrange("p (h e) -> p h e", h=16),
                                              ci_t_hd, wsb, op=ALU.mult)
                ins.then_inc(p.sve, 1); p.inc("sve")
                nc.tensor.wait_ge(p.sve, p.c["sve"])
                lhs = jbig[:, 158 - d_:286 - d_]
                for dc in range(2):
                    ins = nc.tensor.matmul(ps_big[dc][:], lhs,
                                           rb[:, dc * 512:(dc + 1) * 512],
                                           start=False, stop=False)
                ins.then_inc(p.smm, 1); p.inc("smm")
        ins = nc.tensor.matmul(ps_big[0][:], ibig[:, 0:128], rhsring[:, 0, 0:512],
                               start=False, stop=True)
        ins = nc.tensor.matmul(ps_big[1][:], ibig[:, 0:128], rhsring[:, 0, 512:1024],
                               start=False, stop=True)
        ins.then_inc(p.smm, 1); p.inc("smm")
        win_smm = p.c["smm"]
        nc.gpsimd.wait_ge(p.smm, win_smm)
        nc.gpsimd.dma_start(out=ci_tail[:], in_=ci_cur[98:128, :]).then_inc(p.stl, 16)
        p.inc("stl", 16)
        if dbg and i == DBG_TILE:
            nc.gpsimd.wait_ge(p.ssc, p.c["ssc"] + 2)
        nc.scalar.wait_ge(p.smm, win_smm)
        nc.scalar.activation(cv_bf[:, 0:512], ps_big[0][:], AF.Copy)
        ins = nc.scalar.activation(cv_bf[:, 512:1024], ps_big[1][:], AF.Copy)
        ins.then_inc(p.ssc, 1); p.inc("ssc")
        if dbg and i == DBG_TILE:
            nc.gpsimd.dma_start(out=dbg_cv, in_=cv_bf[:]).then_inc(p.sst, 16); p.inc("sst", 16)
        nc.tensor.wait_ge(p.ssc, p.c["ssc"])
        for ds in range(8):
            pt = ps_t[ds % 2]
            nc.tensor.wait_ge(p.ssc, ps_t_free[ds % 2])
            ins = nc.tensor.transpose(pt[:], cv_bf[:, ds * 128:(ds + 1) * 128], p.idt[:])
            ins.then_inc(p.smm, 1); p.inc("smm")
            nc.scalar.wait_ge(p.smm, p.c["smm"])
            ins = nc.scalar.activation(ciT[:, ds, :], pt[:], AF.Copy)
            ins.then_inc(p.ssc, 1); p.inc("ssc")
            ps_t_free[ds % 2] = p.c["ssc"]
        nc.tensor.wait_ge(p.ssc, p.c["ssc"])
        for dc in range(2):
            for s in range(8):
                ins = nc.tensor.matmul(ps_big[dc][:], ciT[:, s, :],
                                       wfcB[:, s, dc * 512:(dc + 1) * 512],
                                       start=(s == 0), stop=(s == 7))
        ins.then_inc(p.smm, 1); p.inc("smm")
        nc.vector.wait_ge(p.smm, p.c["smm"])
        nc.vector.tensor_tensor(bx[:], betb[:], xall[:, i, :], op=ALU.add)
        emit_ln(p, ps_big[0][:], ps_big[1][:], xall[:, i, :], lnbuf, xhat[:], 1e-6)
        if dbg and i == DBG_TILE:
            nc.vector.tensor_copy(dbg_tmp[:, 0:512], ps_big[0][:])
            ins = nc.vector.tensor_copy(dbg_tmp[:, 512:1024], ps_big[1][:])
            ins.then_inc(p.sve, 1); p.inc("sve")
            nc.gpsimd.wait_ge(p.sve, p.c["sve"])
            nc.gpsimd.dma_start(out=dbg_vr, in_=ciT[:].rearrange("p a b -> p (a b)")).then_inc(p.sst, 16); p.inc("sst", 16)
            nc.gpsimd.dma_start(out=dbg_xh, in_=dbg_tmp[:]).then_inc(p.sst, 16); p.inc("sst", 16)
            nc.gpsimd.dma_start(out=dbg_wfcb, in_=wfcB[:].rearrange("p a b -> p (a b)")).then_inc(p.sst, 16); p.inc("sst", 16)
            nc.vector.wait_ge(p.sst, p.c["sst"])
        spacer(nc, p)
        nc.vector.scalar_tensor_tensor(lnbuf["r"][:], xhat[:], 1.0, gamb[:],
                                       op0=ALU.mult, op1=ALU.mult)
        spacer(nc, p)
        ins = nc.vector.tensor_tensor(r2[:], lnbuf["r"][:], bx[:], op=ALU.add)
        ins.then_inc(p.sve, 1); p.inc("sve")
        emit_ln_core(p, r2[:], lnbuf, yout[:], 1e-5)
        nc.gpsimd.wait_ge(p.sve, p.c["sve"])
        dst = x_out[i * 128:(i + 1) * 128, :]
        nc.gpsimd.dma_start(out=dst, in_=yout[:]).then_inc(p.sst, 16)
        p.inc("sst", 16)

    nc.sync.wait_ge(p.sst, p.c["sst"])
    return p


def build_l1attn(dbg=False):
    p = P()
    nc = p.nc
    PC = 512
    NPC = T // PC
    SC = 128
    NSC = PC // SC

    x_in = nc.dram_tensor("x_in", [T, D], F32, kind="ExternalInput").ap()
    wqT_h = nc.dram_tensor("wqT", [8, 128, D], F32, kind="ExternalInput").ap()
    wkT_h = nc.dram_tensor("wkT", [8, 128, D], F32, kind="ExternalInput").ap()
    wvT_h = nc.dram_tensor("wvT", [8, 128, D], F32, kind="ExternalInput").ap()
    wgT_h = nc.dram_tensor("wgT", [8, 128, D], F32, kind="ExternalInput").ap()
    woT_h = nc.dram_tensor("woT", [8, 128, D], F32, kind="ExternalInput").ap()
    waT_h = nc.dram_tensor("waT", [8, 128, 128], F32, kind="ExternalInput").ap()
    wbT_h = nc.dram_tensor("wbT", [8, 128, 128], F32, kind="ExternalInput").ap()
    cw_h = nc.dram_tensor("cw", [12, 1, D], F32, kind="ExternalInput").ap()
    cb_h = nc.dram_tensor("cb", [128, 8, 4], F32, kind="ExternalInput").ap()
    barep_h = nc.dram_tensor("barep", [128, 2], F32, kind="ExternalInput").ap()
    oneblk2_h = nc.dram_tensor("oneblk2", [128, 2], BF16, kind="ExternalInput").ap()
    repsel_h = nc.dram_tensor("repsel", [16, 128], BF16, kind="ExternalInput").ap()
    ident_h = nc.dram_tensor("ident", [128, 128], BF16, kind="ExternalInput").ap()
    ones1_h = nc.dram_tensor("ones1", [1, 128], BF16, kind="ExternalInput").ap()
    onesf_h = nc.dram_tensor("onesf", [128, 1], F32, kind="ExternalInput").ap()
    onescol_h = nc.dram_tensor("onescol", [128, 1], BF16, kind="ExternalInput").ap()
    onesrf_h = nc.dram_tensor("onesrf", [1, 128], F32, kind="ExternalInput").ap()
    x_out = nc.dram_tensor("x_out", [T, D], F32, kind="ExternalOutput").ap()
    wq_s = nc.dram_tensor("wq_s", [4, 8, 128, D], BF16, kind="Internal").ap()
    wk_s = nc.dram_tensor("wk_s", [4, 8, 128, D], BF16, kind="Internal").ap()
    wv_s = nc.dram_tensor("wv_s", [4, 8, 128, D], BF16, kind="Internal").ap()
    wg_s = nc.dram_tensor("wg_s", [8, 128, D], BF16, kind="Internal").ap()
    wo_s = nc.dram_tensor("wo_s", [8, 128, D], BF16, kind="Internal").ap()
    NSCR = 16 * 64 * SC
    kscr = nc.dram_tensor("kscr", [NSCR], BF16, kind="Internal").ap()
    qscr = nc.dram_tensor("qscr", [NSCR], BF16, kind="Internal").ap()
    vscr = nc.dram_tensor("vscr", [NSCR], BF16, kind="Internal").ap()
    oscr = nc.dram_tensor("oscr", [NSCR], BF16, kind="Internal").ap()
    if dbg:
        dbg_q = nc.dram_tensor("dbg_q", [128, 8, PC], BF16, kind="ExternalOutput").ap()
        dbg_k = nc.dram_tensor("dbg_k", [128, 8, PC], BF16, kind="ExternalOutput").ap()
        dbg_ab = nc.dram_tensor("dbg_ab", [128, 2, PC], F32, kind="ExternalOutput").ap()
        dbg_r = nc.dram_tensor("dbg_r", [128, PC], F32, kind="ExternalOutput").ap()
        dbg_krep = nc.dram_tensor("dbg_krep", [128, 64, SC], BF16, kind="ExternalOutput").ap()
        dbg_o = nc.dram_tensor("dbg_o", [128, 8, SC], BF16, kind="ExternalOutput").ap()
        dbg_y = nc.dram_tensor("dbg_y", [128, 8, PC], BF16, kind="ExternalOutput").ap()

    alloc_common(p)
    lnbuf = alloc_lnbuf(p)
    stage = lnbuf["r"]
    xall = p.sbuf("xall", [128, NT, D], BF16)
    xTb = [p.sbuf(f"xTb{i}", [128, 8, PC + 3], BF16) for i in range(2)]
    wtile = [p.sbuf(f"wtile{i}", [128, D], BF16) for i in range(2)]
    cbt = p.sbuf("cbt", [128, 8, 4], F32)
    barep = p.sbuf("barep_s", [128, 2], F32)
    waB = p.sbuf("waB", [128, 8, 128], BF16)
    wbB = p.sbuf("wbB", [128, 8, 128], BF16)
    oneblk2 = p.sbuf("oneblk2_s", [128, 2], BF16)
    qT = p.sbuf("qT", [128, 8, PC], BF16)
    kT = p.sbuf("kT", [128, 8, PC], BF16)
    vT = p.sbuf("vT", [128, 8, PC], BF16)
    gT = p.sbuf("gT", [128, 8, PC], BF16)
    ksq = p.sbuf("ksq", [128, PC], BF16)
    nrm = p.sbuf("nrm", [128, 64], F32)
    rtok = p.sbuf("rtok", [128, 64], F32)
    rtokb = p.sbuf("rtokb", [128, 64], BF16)
    rtT = p.sbuf("rtT", [128, 128], BF16)
    rpad = p.sbuf("rpad", [128, 128], BF16)
    repsel = p.sbuf("repsel_s", [16, 128], BF16)
    a_rep = p.sbuf("a_rep", [128, PC], F32)
    b_rep = p.sbuf("b_rep", [128, PC], F32)
    r_rep = p.sbuf("r_rep", [128, PC], F32)
    br_rep = p.sbuf("br_rep", [128, PC], F32)
    s1_rep = p.sbuf("s1_rep", [128, PC], F32)
    tmpP = p.sbuf("tmpP", [128, PC], F32)
    k_rep = p.sbuf("k_rep", [128, 64, SC], BF16)
    q_rep = p.sbuf("q_rep", [128, 64, SC], BF16)
    v_scan = p.sbuf("v_scan", [128, 8, SC], BF16)
    vbr = p.sbuf("vbr", [128, 8, SC], F32)
    o_chunk = p.sbuf("o_chunk", [128, 8, SC], BF16)
    S_ = p.sbuf("S_", [128, 8, 64], F32)
    Ma = p.sbuf("Ma", [128, 4, 64], F32)
    Mb = p.sbuf("Mb", [128, 4, 64], F32)
    T1a = p.sbuf("T1a", [128, 4, 64], F32)
    T1b = p.sbuf("T1b", [128, 4, 64], F32)
    Spa = p.sbuf("Spa", [128, 4, 64], F32)
    Spb = p.sbuf("Spb", [128, 4, 64], F32)
    da = p.sbuf("da", [128, 4], F32)
    db = p.sbuf("db", [128, 4], F32)
    u2a = p.sbuf("u2a", [128, 4], F32)
    u2b = p.sbuf("u2b", [128, 4], F32)
    yout = p.sbuf("yout_l1", [128, D], F32)

    ps8 = [p.psum(f"ps8_{i}", [128, 512], F32) for i in range(8)]

    blk = p.es.enter_context(nc.Block())
    load_consts(p, ident_h, ones1_h, onesf_h, onescol_h, onesrf_h)
    for dst, srch in [(cbt[:], cb_h), (barep[:], barep_h), (oneblk2[:], oneblk2_h),
                      (repsel[:], repsel_h)]:
        nc.sync.dma_start(out=dst, in_=srch).then_inc(p.sld, 16)
        p.inc("sld", 16)
    for part in range(12):
        nc.sync.wait_ge(p.sve, p.c["sve"])
        nc.sync.dma_start(out=stage[0:1, :], in_=cw_h[part]).then_inc(p.sld, 16)
        p.inc("sld", 16)
        nc.tensor.wait_ge(p.sld, p.c["sld"])
        if part > 0:
            nc.tensor.wait_ge(p.sve, p.c["sve"])
        nc.tensor.matmul(ps8[0][:], p.onesrf[:], stage[0:1, 0:512], start=True, stop=True)
        ins = nc.tensor.matmul(ps8[1][:], p.onesrf[:], stage[0:1, 512:1024], start=True, stop=True)
        ins.then_inc(p.smm, 1); p.inc("smm")
        nc.vector.wait_ge(p.smm, p.c["smm"])
        nc.vector.tensor_copy(xall[:, part, 0:512], ps8[0][:])
        ins = nc.vector.tensor_copy(xall[:, part, 512:1024], ps8[1][:])
        ins.then_inc(p.sve, 1); p.inc("sve")
    for s in range(8):
        nc.sync.wait_ge(p.sve, p.c["sve"])
        nc.sync.dma_start(out=stage[:, 0:128], in_=waT_h[s]).then_inc(p.sld, 16); p.inc("sld", 16)
        nc.sync.dma_start(out=stage[:, 128:256], in_=wbT_h[s]).then_inc(p.sld, 16); p.inc("sld", 16)
        nc.vector.wait_ge(p.sld, p.c["sld"])
        nc.vector.tensor_copy(waB[:, s, :], stage[:, 0:128])
        ins = nc.vector.tensor_copy(wbB[:, s, :], stage[:, 128:256])
        ins.then_inc(p.sve, 1); p.inc("sve")

    def prep_stacked(w_h, w_scr, cw0):
        for s in range(8):
            nc.sync.wait_ge(p.sve, p.c["sve"])
            nc.sync.wait_ge(p.sst, p.c["sst"])
            nc.sync.dma_start(out=stage[:], in_=w_h[s]).then_inc(p.sld, 16); p.inc("sld", 16)
            nc.vector.wait_ge(p.sld, p.c["sld"])
            for kk in range(4):
                if kk >= 2:
                    nc.vector.wait_ge(p.sst, p.c["sst"] - 16)
                ins = nc.vector.tensor_tensor(wtile[kk % 2][:], stage[:],
                                              xall[:, cw0 + kk, :], op=ALU.mult)
                ins.then_inc(p.sve, 1); p.inc("sve")
                nc.gpsimd.wait_ge(p.sve, p.c["sve"])
                nc.gpsimd.dma_start(out=w_scr[kk, s], in_=wtile[kk % 2][:])\
                    .then_inc(p.sst, 16)
                p.inc("sst", 16)

    prep_stacked(wqT_h, wq_s, 0)
    prep_stacked(wkT_h, wk_s, 4)
    prep_stacked(wvT_h, wv_s, 8)

    def prep_plain(w_h, w_scr):
        for s in range(8):
            nc.sync.wait_ge(p.sve, p.c["sve"])
            nc.sync.wait_ge(p.sst, p.c["sst"])
            nc.sync.dma_start(out=stage[:], in_=w_h[s]).then_inc(p.sld, 16); p.inc("sld", 16)
            nc.vector.wait_ge(p.sld, p.c["sld"])
            ins = nc.vector.tensor_copy(wtile[s % 2][:], stage[:])
            ins.then_inc(p.sve, 1); p.inc("sve")
            nc.gpsimd.wait_ge(p.sve, p.c["sve"])
            nc.gpsimd.dma_start(out=w_scr[s], in_=wtile[s % 2][:]).then_inc(p.sst, 16)
            p.inc("sst", 16)

    prep_plain(wgT_h, wg_s)
    prep_plain(woT_h, wo_s)

    nc.vector.memset(xTb[0][:], 0.0)
    nc.vector.memset(xTb[1][:], 0.0)
    ins = nc.vector.memset(S_[:], 0.0)
    ins.then_inc(p.sve, 1); p.inc("sve")
    nc.sync.wait_ge(p.sve, p.c["sve"])
    nc.sync.wait_ge(p.sst, p.c["sst"])
    for tl in range(NT):
        nc.sync.wait_ge(p.ssc, p.c["ssc"])
        nc.sync.dma_start(out=stage[:], in_=x_in[tl * 128:(tl + 1) * 128, :])\
            .then_inc(p.sld, 16)
        p.inc("sld", 16)
        nc.scalar.wait_ge(p.sld, p.c["sld"])
        ins = nc.scalar.activation(xall[:, tl, :], stage[:], AF.Copy)
        ins.then_inc(p.ssc, 1); p.inc("ssc")
    all_loaded = p.c["ssc"]

    for pc in range(NPC):
        xb = xTb[pc % 2]
        xprev = xTb[1 - pc % 2]
        nc.sync.wait_ge(p.ssc, all_loaded)
        if pc > 0:
            nc.sync.wait_ge(p.smm, p.c["smm"])
        for tl4 in range(PC // 128):
            tl = pc * (PC // 128) + tl4
            for ds in range(8):
                nc.sync.dma_start(out=xb[:, ds, 3 + tl4 * 128:3 + (tl4 + 1) * 128],
                                  in_=xall[:, tl, ds * 128:(ds + 1) * 128],
                                  transpose=True).then_inc(p.sld, 16)
                p.inc("sld", 16)
        if pc > 0:
            nc.gpsimd.wait_ge(p.smm, p.c["smm"])
            nc.gpsimd.dma_start(out=xb[:, :, 0:3], in_=xprev[:, :, PC:PC + 3])\
                .then_inc(p.sgp, 16)
            p.inc("sgp", 16)
            nc.tensor.wait_ge(p.sgp, p.c["sgp"])
        xT_ld = p.c["sld"]
        nc.tensor.wait_ge(p.sld, xT_ld)

        def proj_stacked(w_scr, dstT, cb_col):
            for kk in range(4):
                for s in range(8):
                    wt = wtile[(kk * 8 + s) % 2]
                    if kk * 8 + s >= 2:
                        nc.sync.wait_ge(p.smm, p.c["smm"] - 1)
                    nc.sync.dma_start(out=wt[:], in_=w_scr[kk, s]).then_inc(p.sld, 16)
                    p.inc("sld", 16)
                    nc.tensor.wait_ge(p.sld, p.c["sld"])
                    if kk == 0 and s == 0:
                        nc.tensor.wait_ge(p.ssc, p.c["ssc"])
                    for cs in range(8):
                        ins = nc.tensor.matmul(
                            ps8[cs][:], wt[:, cs * 128:(cs + 1) * 128],
                            xb[:, s, kk:kk + PC], start=(kk == 0 and s == 0),
                            stop=(kk == 3 and s == 7))
                    ins.then_inc(p.smm, 1); p.inc("smm")
            nc.scalar.wait_ge(p.smm, p.c["smm"])
            for cs in range(8):
                ins = nc.scalar.activation(dstT[:, cs, :], ps8[cs][:], AF.Silu,
                                           bias=cbt[:, cs, cb_col:cb_col + 1], scale=1.0)
            ins.then_inc(p.ssc, 1); p.inc("ssc")

        proj_stacked(wq_s, qT, 0)
        proj_stacked(wk_s, kT, 1)
        proj_stacked(wv_s, vT, 2)
        for s in range(8):
            wt = wtile[s % 2]
            if s >= 2:
                nc.sync.wait_ge(p.smm, p.c["smm"] - 1)
            nc.sync.dma_start(out=wt[:], in_=wg_s[s]).then_inc(p.sld, 16)
            p.inc("sld", 16)
            nc.tensor.wait_ge(p.sld, p.c["sld"])
            if s == 0:
                nc.tensor.wait_ge(p.ssc, p.c["ssc"])
            for cs in range(8):
                ins = nc.tensor.matmul(ps8[cs][:], wt[:, cs * 128:(cs + 1) * 128],
                                       xb[:, s, 3:3 + PC], start=(s == 0), stop=(s == 7))
            ins.then_inc(p.smm, 1); p.inc("smm")
        nc.scalar.wait_ge(p.smm, p.c["smm"])
        for cs in range(8):
            ins = nc.scalar.activation(gT[:, cs, :], ps8[cs][:], AF.Silu,
                                       bias=cbt[:, cs, 3:4], scale=1.0)
        ins.then_inc(p.ssc, 1); p.inc("ssc")
        nc.tensor.wait_ge(p.ssc, p.c["ssc"])
        for s in range(8):
            nc.tensor.matmul(ps8[0][:], waB[:, s, :], xb[:, s, 3:3 + PC],
                             start=(s == 0), stop=(s == 7))
            ins = nc.tensor.matmul(ps8[1][:], wbB[:, s, :], xb[:, s, 3:3 + PC],
                                   start=(s == 0), stop=(s == 7))
        ins.then_inc(p.smm, 1); p.inc("smm")
        nc.scalar.wait_ge(p.smm, p.c["smm"])
        nc.scalar.activation(a_rep[:], ps8[0][:], AF.Sigmoid, bias=barep[:, 0:1], scale=1.0)
        ins = nc.scalar.activation(b_rep[:], ps8[1][:], AF.Sigmoid, bias=barep[:, 1:2], scale=1.0)
        ins.then_inc(p.ssc, 1); p.inc("ssc")
        ab_done = p.c["ssc"]
        for cs in range(8):
            nc.scalar.wait_ge(p.ssc, p.c["ssc"])
            ins = nc.scalar.activation(ksq[:], kT[:, cs, :], AF.Square)
            ins.then_inc(p.ssc, 1); p.inc("ssc")
            nc.tensor.wait_ge(p.ssc, p.c["ssc"])
            for tl4 in range(4):
                ins = nc.tensor.matmul(ps8[2][:, tl4 * 16 + cs * 2:tl4 * 16 + cs * 2 + 2],
                                       ksq[:, tl4 * 128:(tl4 + 1) * 128],
                                       oneblk2[:], start=True, stop=True)
            ins.then_inc(p.smm, 1); p.inc("smm")
        nc.scalar.wait_ge(p.smm, p.c["smm"])
        ins = nc.scalar.activation(nrm[:], ps8[2][:, 0:64], AF.Sqrt)
        ins.then_inc(p.ssc, 1); p.inc("ssc")
        nc.vector.wait_ge(p.ssc, p.c["ssc"])
        nc.vector.tensor_scalar_max(rtok[:], nrm[:], 1e-12)
        spacer(nc, p)
        nc.vector.reciprocal(rtok[:], rtok[:])
        spacer(nc, p)
        ins = nc.vector.tensor_copy(rtokb[:], rtok[:])
        ins.then_inc(p.sve, 1); p.inc("sve")
        for tl4 in range(4):
            nc.vector.wait_ge(p.sld, p.c["sld"])
            ins = nc.vector.tensor_copy(rpad[:, 0:16], rtokb[:, tl4 * 16:(tl4 + 1) * 16])
            ins.then_inc(p.sve, 1); p.inc("sve")
            nc.sync.wait_ge(p.sve, p.c["sve"])
            nc.sync.dma_start(out=rtT[:], in_=rpad[:], transpose=True).then_inc(p.sld, 16)
            p.inc("sld", 16)
            nc.tensor.wait_ge(p.sld, p.c["sld"])
            ins = nc.tensor.matmul(ps8[3][:, 0:128],
                                   repsel[:], rtT[0:16, :], start=True, stop=True)
            ins.then_inc(p.smm, 1); p.inc("smm")
            nc.vector.wait_ge(p.smm, p.c["smm"])
            ins = nc.vector.tensor_copy(r_rep[:, tl4 * 128:(tl4 + 1) * 128], ps8[3][:, 0:128])
            ins.then_inc(p.sve, 1); p.inc("sve")
        nc.vector.wait_ge(p.ssc, ab_done)
        nc.vector.tensor_tensor(br_rep[:], b_rep[:], r_rep[:], op=ALU.mult)
        spacer(nc, p)
        nc.vector.tensor_tensor(tmpP[:], br_rep[:], r_rep[:], op=ALU.mult)
        spacer(nc, p)
        nc.vector.tensor_tensor(s1_rep[:], tmpP[:], a_rep[:], op=ALU.mult)
        spacer(nc, p)
        ins = nc.vector.tensor_scalar_mul(s1_rep[:], s1_rep[:], -1.0)
        ins.then_inc(p.sve, 1); p.inc("sve")
        scal_done = p.c["sve"]

        for sc in range(NSC):
            t0 = sc * SC
            nc.gpsimd.wait_ge(p.ssc, ab_done)
            if pc + sc > 0:
                nc.gpsimd.wait_ge(p.sve, p.c["sve"])
            kv = kT[:, :, t0:t0 + SC]
            qv = qT[:, :, t0:t0 + SC]
            vv = vT[:, :, t0:t0 + SC]
            wr = lambda scr, srcv: nc.gpsimd.dma_start(
                out=scr.rearrange("(c pp t) -> pp c t", c=8, pp=128), in_=srcv)
            wr(kscr, kv).then_inc(p.sgp, 16); p.inc("sgp", 16)
            wr(qscr, qv).then_inc(p.sgp, 16); p.inc("sgp", 16)
            wr(vscr, vv).then_inc(p.sgp, 16); p.inc("sgp", 16)
            nc.sync.wait_ge(p.sgp, p.c["sgp"])
            for h in range(16):
                base = h * 64 * SC
                srcb = kscr[base:base + 64 * SC].rearrange("(j t) -> j t", j=64)\
                    .unsqueeze(0).broadcast_to([8, 64, SC])
                nc.sync.dma_start(out=k_rep[h * 8:(h + 1) * 8, :, :], in_=srcb)\
                    .then_inc(p.sld, 16)
                p.inc("sld", 16)
                srcb = qscr[base:base + 64 * SC].rearrange("(j t) -> j t", j=64)\
                    .unsqueeze(0).broadcast_to([8, 64, SC])
                nc.sync.dma_start(out=q_rep[h * 8:(h + 1) * 8, :, :], in_=srcb)\
                    .then_inc(p.sld, 16)
                p.inc("sld", 16)
            nc.sync.dma_start(out=v_scan[:],
                              in_=vscr.rearrange("(pp i t) -> pp i t", pp=128, i=8))\
                .then_inc(p.sld, 16)
            p.inc("sld", 16)
            scan_ld = p.c["sld"]
            nc.vector.wait_ge(p.sld, scan_ld)
            nc.vector.wait_ge(p.sve, scal_done)
            brb = br_rep[:, t0:t0 + SC].unsqueeze(1).broadcast_to([128, 8, SC])
            ins = nc.vector.tensor_tensor(vbr[:], v_scan[:], brb, op=ALU.mult)
            ins.then_inc(p.sve, 1); p.inc("sve")
            for t in range(SC):
                k_b = k_rep[:, :, t].unsqueeze(1).broadcast_to([128, 4, 64])
                q_b = q_rep[:, :, t].unsqueeze(1).broadcast_to([128, 4, 64])
                at = a_rep[:, t0 + t:t0 + t + 1]
                s1t = s1_rep[:, t0 + t:t0 + t + 1]
                nc.vector.tensor_tensor(Ma[:], S_[:, 0:4, :], k_b, op=ALU.mult)
                nc.vector.tensor_tensor(Mb[:], S_[:, 4:8, :], k_b, op=ALU.mult)
                nc.vector.tensor_reduce(da[:].unsqueeze(-1), Ma[:], axis=AX.X, op=ALU.add)
                nc.vector.tensor_reduce(db[:].unsqueeze(-1), Mb[:], axis=AX.X, op=ALU.add)
                nc.vector.scalar_tensor_tensor(u2a[:], da[:], s1t, vbr[:, 0:4, t],
                                               op0=ALU.mult, op1=ALU.add)
                nc.vector.scalar_tensor_tensor(u2b[:], db[:], s1t, vbr[:, 4:8, t],
                                               op0=ALU.mult, op1=ALU.add)
                spacer(nc, p)
                nc.vector.tensor_tensor(T1a[:], u2a[:].unsqueeze(-1).broadcast_to([128, 4, 64]),
                                        k_b, op=ALU.mult)
                nc.vector.tensor_tensor(T1b[:], u2b[:].unsqueeze(-1).broadcast_to([128, 4, 64]),
                                        k_b, op=ALU.mult)
                nc.vector.scalar_tensor_tensor(Spa[:], S_[:, 0:4, :], at, T1a[:],
                                               op0=ALU.mult, op1=ALU.add)
                nc.vector.scalar_tensor_tensor(Spb[:], S_[:, 4:8, :], at, T1b[:],
                                               op0=ALU.mult, op1=ALU.add)
                nc.vector.tensor_scalar(S_[:, 0:4, :], Spa[:], 5.0, -5.0,
                                        op0=ALU.min, op1=ALU.max)
                nc.vector.tensor_scalar(S_[:, 4:8, :], Spb[:], 5.0, -5.0,
                                        op0=ALU.min, op1=ALU.max)
                nc.vector.tensor_tensor(Ma[:], S_[:, 0:4, :], q_b, op=ALU.mult)
                nc.vector.tensor_tensor(Mb[:], S_[:, 4:8, :], q_b, op=ALU.mult)
                with nc.allow_low_precision(reason="DVE reduce accumulates fp32 internally; bf16 store only"):
                    nc.vector.tensor_reduce(o_chunk[:, 0:4, t:t + 1], Ma[:], axis=AX.X, op=ALU.add)
                    ins = nc.vector.tensor_reduce(o_chunk[:, 4:8, t:t + 1], Mb[:], axis=AX.X, op=ALU.add)
            ins.then_inc(p.sve, 1); p.inc("sve")
            nc.gpsimd.wait_ge(p.sve, p.c["sve"])
            nc.gpsimd.dma_start(out=oscr.rearrange("(pp i t) -> pp i t", pp=128, i=8),
                                in_=o_chunk[:]).then_inc(p.sgp, 16)
            p.inc("sgp", 16)
            nc.sync.wait_ge(p.sgp, p.c["sgp"])
            nc.sync.dma_start(out=qT[:, :, t0:t0 + SC],
                              in_=oscr.rearrange("(c pp t) -> pp c t", c=8, pp=128))\
                .then_inc(p.sld, 16)
            p.inc("sld", 16)
        scan_done_ld = p.c["sld"]
        nc.vector.wait_ge(p.sld, scan_done_ld)
        for cs in range(8):
            ins = nc.vector.tensor_tensor(qT[:, cs, :], qT[:, cs, :], gT[:, cs, :],
                                          op=ALU.mult)
        ins.then_inc(p.sve, 1); p.inc("sve")
        y_done = p.c["sve"]
        for s in range(8):
            wt = wtile[s % 2]
            if s >= 2:
                nc.sync.wait_ge(p.smm, p.c["smm"] - 1)
            nc.sync.dma_start(out=wt[:], in_=wo_s[s]).then_inc(p.sld, 16)
            p.inc("sld", 16)
            nc.tensor.wait_ge(p.sld, p.c["sld"])
            if s == 0:
                nc.tensor.wait_ge(p.sve, y_done)
                nc.tensor.wait_ge(p.ssc, p.c["ssc"])
            for tl4 in range(4):
                for dc in range(2):
                    ins = nc.tensor.matmul(
                        ps8[tl4 * 2 + dc][:],
                        qT[:, s, tl4 * 128:(tl4 + 1) * 128],
                        wt[:, dc * 512:(dc + 1) * 512],
                        start=(s == 0), stop=(s == 7))
            ins.then_inc(p.smm, 1); p.inc("smm")
        nc.vector.wait_ge(p.smm, p.c["smm"])
        for tl4 in range(4):
            tl = pc * 4 + tl4
            emit_ln(p, ps8[tl4 * 2][:], ps8[tl4 * 2 + 1][:],
                    xall[:, tl, :], lnbuf, yout[:], 1e-5)
            nc.gpsimd.wait_ge(p.sve, p.c["sve"])
            nc.gpsimd.dma_start(out=x_out[tl * 128:(tl + 1) * 128, :], in_=yout[:])\
                .then_inc(p.sst, 16)
            p.inc("sst", 16)
            nc.vector.wait_ge(p.sst, p.c["sst"])
    nc.sync.wait_ge(p.sst, p.c["sst"])
    return p


_PROGS = {}

def _get_prog(name):
    if name not in _PROGS:
        if name == "l0attn":
            pp = build_l0attn(); _PROGS[name] = pp.close()
        elif name == "l0ffn":
            pp = build_l0ffn(); _PROGS[name] = pp.close()
        elif name == "l1attn":
            pp = build_l1attn(); _PROGS[name] = pp.close()
        elif name == "l1ffn":
            pp = build_l1ffn(); _PROGS[name] = pp.close()
    return _PROGS[name]


def _consts():
    c = {
        "ident": np.eye(128, dtype=ml_dtypes.bfloat16),
        "ones1": np.ones((1, 128), dtype=ml_dtypes.bfloat16),
        "onesf": np.ones((128, 1), dtype=np.float32),
        "onescol": np.ones((128, 1), dtype=ml_dtypes.bfloat16),
        "onesrf": np.ones((1, 128), dtype=np.float32),
    }
    return c


def kernel(
    L0_attn_x, L0_attn_W_pe, L0_attn_b_pe, L0_attn_stt_energy, L0_attn_W_fc,
    L0_attn_gamma, L0_attn_beta, L0_ffn_W1, L0_ffn_b1, L0_ffn_W2, L0_ffn_b2,
    L1_attn_Wq, L1_attn_Wk, L1_attn_Wv, L1_attn_Wa, L1_attn_ba, L1_attn_Wb,
    L1_attn_bb, L1_attn_conv_q_w, L1_attn_conv_q_b, L1_attn_conv_k_w,
    L1_attn_conv_k_b, L1_attn_conv_v_w, L1_attn_conv_v_b, L1_attn_Wg,
    L1_attn_bg, L1_attn_Wo, L1_ffn_w_gate, L1_ffn_w_up, L1_ffn_w_down,
):
    B = 8
    x = np.ascontiguousarray(np.asarray(L0_attn_x, dtype=np.float32))
    C = _consts()

    W_pe = np.asarray(L0_attn_W_pe, np.float32)
    b_pe = np.asarray(L0_attn_b_pe, np.float32)
    stt = np.asarray(L0_attn_stt_energy, np.float32).reshape(16, 31)
    W_fc = np.asarray(L0_attn_W_fc, np.float32)
    gamma = np.asarray(L0_attn_gamma, np.float32)
    beta = np.asarray(L0_attn_beta, np.float32)
    wpe_blk = np.zeros((128, 124), np.float32)
    wpe_blk[0:64, 0:62] = W_pe
    wpe_blk[64:128, 62:124] = W_pe
    bpe_row = np.zeros((1, 1024), np.float32)
    for s in range(8):
        for hp in range(2):
            bpe_row[0, s * 128 + hp * 62:s * 128 + hp * 62 + 62] = b_pe
    stt_row = np.ascontiguousarray(stt.T).reshape(1, 496)
    recip = np.zeros((128, 32), np.float32)
    for tl in range(32):
        recip[:, tl] = 1.0 / (tl * 128 + np.arange(128) + 1)
    mask0 = np.zeros((128, 496), np.float32)
    for t in range(128):
        for k in range(31):
            if t + k >= 30:
                mask0[t, k * 16 + np.arange(16)] = 1.0
    ibig = np.zeros((128, 260), np.float32)
    for ppp in range(128):
        if ppp + 128 < 260:
            ibig[ppp, ppp + 128] = 1.0
    jbig = np.zeros((30, 286), np.float32)
    for jj in range(30):
        jbig[jj, jj + 128] = 1.0
    m0 = {
        "tri": np.triu(np.ones((128, 128))).astype(ml_dtypes.bfloat16),
        "wpe": wpe_blk, "bpe": bpe_row, "stt": stt_row, "recip": recip,
        "mask0": mask0,
        "wfcT": np.ascontiguousarray(W_fc.T).reshape(8, 128, 1024),
        "gam": gamma.reshape(1, 1024), "bet": beta.reshape(1, 1024),
        "ibig": ibig.astype(ml_dtypes.bfloat16),
        "jbig": jbig.astype(ml_dtypes.bfloat16),
    }
    m0.update(C)
    nc0 = _get_prog("l0attn")
    maps = [dict(m0, x_in=np.ascontiguousarray(x[b])) for b in range(B)]
    res = run_bass_kernel_spmd(nc0, maps, core_ids=list(range(B)))
    x1 = [res.results[b]["x_out"] for b in range(B)]

    W1 = np.asarray(L0_ffn_W1, np.float32)
    b1 = np.asarray(L0_ffn_b1, np.float32)
    W2 = np.asarray(L0_ffn_W2, np.float32)
    b2 = np.asarray(L0_ffn_b2, np.float32)
    m1 = {
        "w1t": np.ascontiguousarray(W1.T).reshape(8, 128, 4096),
        "b1a": np.ascontiguousarray(b1.reshape(32, 128).T),
        "w2t": np.ascontiguousarray(W2.T).reshape(32, 128, 1024),
        "b2r": b2.reshape(1, 1024),
    }
    m1.update(C)
    nc1 = _get_prog("l0ffn")
    maps = [dict(m1, x_in=np.ascontiguousarray(x1[b])) for b in range(B)]
    res = run_bass_kernel_spmd(nc1, maps, core_ids=list(range(B)))
    x2 = [res.results[b]["x_out"] for b in range(B)]

    Wq = np.asarray(L1_attn_Wq, np.float32); Wk = np.asarray(L1_attn_Wk, np.float32)
    Wv = np.asarray(L1_attn_Wv, np.float32); Wg = np.asarray(L1_attn_Wg, np.float32)
    Wo = np.asarray(L1_attn_Wo, np.float32)
    Wa = np.asarray(L1_attn_Wa, np.float32); Wb = np.asarray(L1_attn_Wb, np.float32)
    ba = np.asarray(L1_attn_ba, np.float32); bb = np.asarray(L1_attn_bb, np.float32)
    cqw = np.asarray(L1_attn_conv_q_w, np.float32); cqb = np.asarray(L1_attn_conv_q_b, np.float32)
    ckw = np.asarray(L1_attn_conv_k_w, np.float32); ckb = np.asarray(L1_attn_conv_k_b, np.float32)
    cvw = np.asarray(L1_attn_conv_v_w, np.float32); cvb = np.asarray(L1_attn_conv_v_b, np.float32)
    bg = np.asarray(L1_attn_bg, np.float32)

    def wT8(W):
        return np.ascontiguousarray(W.T).reshape(8, 128, 1024)
    waT = np.zeros((1024, 128), np.float32)
    wbT = np.zeros((1024, 128), np.float32)
    for h in range(16):
        for i8 in range(8):
            waT[:, h * 8 + i8] = Wa[h]
            wbT[:, h * 8 + i8] = Wb[h]
    cw = np.zeros((12, 1, 1024), np.float32)
    for j, w3 in enumerate([cqw, ckw, cvw]):
        for kk in range(4):
            cw[j * 4 + kk, 0, :] = w3[:, 0, kk]
    cb = np.zeros((128, 8, 4), np.float32)
    for ss in range(8):
        cb[:, ss, 0] = cqb[ss * 128:(ss + 1) * 128]
        cb[:, ss, 1] = ckb[ss * 128:(ss + 1) * 128]
        cb[:, ss, 2] = cvb[ss * 128:(ss + 1) * 128]
        cb[:, ss, 3] = bg[ss * 128:(ss + 1) * 128]
    barep = np.zeros((128, 2), np.float32)
    barep[:, 0] = np.repeat(ba, 8)
    barep[:, 1] = np.repeat(bb, 8)
    oneblk2 = np.zeros((128, 2), np.float32)
    oneblk2[0:64, 0] = 1.0
    oneblk2[64:128, 1] = 1.0
    repsel = np.zeros((16, 128), np.float32)
    for h in range(16):
        repsel[h, h * 8:(h + 1) * 8] = 1.0
    m2 = {
        "wqT": wT8(Wq), "wkT": wT8(Wk), "wvT": wT8(Wv), "wgT": wT8(Wg), "woT": wT8(Wo),
        "waT": np.ascontiguousarray(waT).reshape(8, 128, 128),
        "wbT": np.ascontiguousarray(wbT).reshape(8, 128, 128),
        "cw": cw, "cb": cb, "barep": barep,
        "oneblk2": oneblk2.astype(ml_dtypes.bfloat16),
        "repsel": repsel.astype(ml_dtypes.bfloat16),
    }
    m2.update(C)
    nc2 = _get_prog("l1attn")
    maps = [dict(m2, x_in=np.ascontiguousarray(x2[b])) for b in range(B)]
    res = run_bass_kernel_spmd(nc2, maps, core_ids=list(range(B)))
    x3 = [res.results[b]["x_out"] for b in range(B)]

    wg_i = np.asarray(L1_ffn_w_gate, np.int32)
    wu_i = np.asarray(L1_ffn_w_up, np.int32)
    wd_i = np.asarray(L1_ffn_w_down, np.int32)
    m3 = {
        "wgT": np.ascontiguousarray(wg_i.T).reshape(8, 128, 3072),
        "wuT": np.ascontiguousarray(wu_i.T).reshape(8, 128, 3072),
        "wdT": np.ascontiguousarray(wd_i.T).reshape(24, 128, 1024),
    }
    m3.update(C)
    nc3 = _get_prog("l1ffn")
    maps = [dict(m3, x_in=np.ascontiguousarray(x3[b])) for b in range(B)]
    res = run_bass_kernel_spmd(nc3, maps, core_ids=list(range(B)))
    out = np.stack([res.results[b]["x_out"] for b in range(B)], axis=0)
    return out.astype(np.float32)


# version 02
# speedup vs baseline: 1.2107x; 1.2107x over previous
import numpy as np
import ml_dtypes
from concourse.bass_utils import run_bass_kernel_spmd

"""Block program builders (developed standalone, later fused into kernel.py).

Conventions:
- One core = one batch element (T=4096 tokens, D=1024).
- Stream token-major [T, D] fp32 in HBM (v1 unfused).
- Engines emitted interleaved in dataflow order; cross-engine order via
  semaphores with build-time counters (p.c). Same-engine DVE RAW pairs need
  >=1 independent DVE op in between (verified hardware hazard).
- Matmuls: bf16 operands, fp32 PSUM accumulation.
"""
import contextlib
import numpy as np
import concourse.bass as bass
import concourse.mybir as mybir

F32 = mybir.dt.float32
BF16 = mybir.dt.bfloat16
I32 = mybir.dt.int32
AF = mybir.ActivationFunctionType
ALU = mybir.AluOpType
AX = mybir.AxisListType

T = 4096
D = 1024
NT = T // 128


class P:
    def __init__(self):
        self.nc = bass.Bass("TRN2", target_bir_lowering=False)
        self.es = contextlib.ExitStack()
        self.c = {}

    def sbuf(self, name, shape, dt):
        return self.es.enter_context(self.nc.sbuf_tensor(name, shape, dt))

    def psum(self, name, shape, dt=F32):
        return self.es.enter_context(self.nc.psum_tensor(name, shape, dt))

    def sem(self, name):
        s = self.es.enter_context(self.nc.semaphore(name))
        self.c[name] = 0
        setattr(self, name, s)
        return s

    def inc(self, name, by=1):
        self.c[name] += by
        return self.c[name]

    def close(self):
        self.es.close()
        return self.nc


def alloc_common(p):
    nc = p.nc
    p.idt = p.sbuf("c_idt", [128, 128], BF16)
    p.ones1 = p.sbuf("c_ones1", [1, 128], BF16)
    p.onescol = p.sbuf("c_onescol", [128, 1], BF16)
    p.onesf = p.sbuf("c_onesf", [128, 1], F32)
    p.onesrf = p.sbuf("c_onesrf", [1, 128], F32)
    p.j1 = p.sbuf("c_j1", [128, 1], F32)
    for s in ["sld", "sst", "smm", "sve", "ssc", "str", "sgp"]:
        p.sem(s)


def spacer(nc, p):
    nc.vector.tensor_copy(p.j1[:], p.onesf[:])


def load_consts(p, ident_h, ones1_h, onesf_h, onescol_h, onesrf_h):
    nc = p.nc
    nc.sync.dma_start(out=p.idt[:], in_=ident_h).then_inc(p.sld, 16); p.inc("sld", 16)
    nc.sync.dma_start(out=p.ones1[:], in_=ones1_h).then_inc(p.sld, 16); p.inc("sld", 16)
    nc.sync.dma_start(out=p.onesf[:], in_=onesf_h).then_inc(p.sld, 16); p.inc("sld", 16)
    nc.sync.dma_start(out=p.onescol[:], in_=onescol_h).then_inc(p.sld, 16); p.inc("sld", 16)
    nc.sync.dma_start(out=p.onesrf[:], in_=onesrf_h).then_inc(p.sld, 16); p.inc("sld", 16)


def tern_prep(p, out_writer, w_hbm, nsl, FREE, stage, stage_i, red, rcp, tiny,
              ps_s, ps_r, n_elem):
    nc = p.nc
    CW = 1024
    ncw = (FREE + CW - 1) // CW
    for s in range(nsl):
        for cw in range(ncw):
            w = min(CW, FREE - cw * CW)
            nc.sync.wait_ge(p.sve, p.c["sve"])
            nc.sync.dma_start(out=stage[:, 0:w], in_=w_hbm[s][:, cw * CW:cw * CW + w])\
                .then_inc(p.sld, 16)
            p.inc("sld", 16)
            nc.vector.wait_ge(p.sld, p.c["sld"])
            op = ALU.add
            if cw == 0:
                ins = nc.vector.tensor_reduce(red[:, s:s + 1], stage[:, 0:w],
                                              axis=AX.X, op=op, apply_absolute_value=True)
            else:
                nc.vector.tensor_reduce(tiny["acc1"][:], stage[:, 0:w], axis=AX.X,
                                        op=op, apply_absolute_value=True)
                spacer(nc, p)
                ins = nc.vector.tensor_tensor(red[:, s:s + 1], red[:, s:s + 1],
                                              tiny["acc1"][:], op=ALU.add)
            ins.then_inc(p.sve, 1); p.inc("sve")
    spacer(nc, p)
    ins = nc.vector.tensor_copy(tiny["redb"][:, 0:nsl], red[:, 0:nsl])
    ins.then_inc(p.sve, 1); p.inc("sve")
    nc.tensor.wait_ge(p.sve, p.c["sve"])
    nc.tensor.matmul(ps_s[0:1, 0:nsl], p.onescol[:], tiny["redb"][:, 0:nsl],
                     start=True, stop=True).then_inc(p.smm, 1)
    p.inc("smm")
    nc.vector.wait_ge(p.smm, p.c["smm"])
    nc.vector.tensor_reduce(tiny["t11"][:], ps_s[0:1, 0:nsl], axis=AX.X, op=ALU.add)
    spacer(nc, p)
    nc.vector.tensor_scalar(tiny["t11"][:], tiny["t11"][:], 1.0 / n_elem, 1e-8,
                            op0=ALU.mult, op1=ALU.max)
    spacer(nc, p)
    ins = nc.vector.reciprocal(tiny["t11b"][:], tiny["t11"][:])
    ins.then_inc(p.sve, 1); p.inc("sve")
    nc.tensor.wait_ge(p.sve, p.c["sve"])
    nc.tensor.matmul(ps_r[:, 0:1], p.onesrf[:], tiny["t11b"][:], start=True,
                     stop=True).then_inc(p.smm, 1)
    p.inc("smm")
    nc.vector.wait_ge(p.smm, p.c["smm"])
    nc.vector.tensor_copy(rcp[:], ps_r[:, 0:1])
    ins = nc.vector.tensor_copy(p.j1[:], p.onesf[:])
    ins.then_inc(p.sve, 1); p.inc("sve")
    for s in range(nsl):
        for cw in range(ncw):
            w = min(CW, FREE - cw * CW)
            nc.sync.wait_ge(p.sve, p.c["sve"])
            nc.sync.dma_start(out=stage[:, 0:w], in_=w_hbm[s][:, cw * CW:cw * CW + w])\
                .then_inc(p.sld, 16)
            p.inc("sld", 16)
            nc.vector.wait_ge(p.sld, p.c["sld"])
            nc.vector.tensor_scalar_mul(stage_i[:, 0:w], stage[:, 0:w], rcp[:])
            spacer(nc, p)
            ins = nc.vector.tensor_scalar(out_writer(s)[:, cw * CW:cw * CW + w],
                                          stage_i[:, 0:w], 1.0, -1.0,
                                          op0=ALU.min, op1=ALU.max)
            ins.then_inc(p.sve, 1); p.inc("sve")


def emit_ln_core(p, r_ap, lnbuf, y_out, eps):
    nc = p.nc
    b = lnbuf
    nc.scalar.wait_ge(p.sve, p.c["sve"])
    nc.scalar.activation(b["sq"][:], r_ap, AF.Square, accum_out=b["ssq"][:])
    ins = nc.scalar.activation(b["sq"][:], r_ap, AF.Identity, accum_out=b["s1"][:])
    ins.then_inc(p.ssc, 1); p.inc("ssc")
    nc.vector.wait_ge(p.ssc, p.c["ssc"])
    nc.vector.tensor_scalar_mul(b["mneg"][:], b["s1"][:], -1.0 / 1024.0)
    nc.vector.tensor_scalar(b["var0"][:], b["ssq"][:], 1.0 / 1024.0, eps,
                            op0=ALU.mult, op1=ALU.add)
    nc.vector.tensor_tensor(b["msq"][:], b["mneg"][:], b["mneg"][:], op=ALU.mult)
    spacer(nc, p)
    ins = nc.vector.scalar_tensor_tensor(b["var"][:], b["msq"][:], -1.0, b["var0"][:],
                                         op0=ALU.mult, op1=ALU.add)
    ins.then_inc(p.sve, 1); p.inc("sve")
    nc.scalar.wait_ge(p.sve, p.c["sve"])
    ins = nc.scalar.activation(b["rstd0"][:], b["var"][:], AF.Sqrt)
    ins.then_inc(p.ssc, 1); p.inc("ssc")
    nc.vector.wait_ge(p.ssc, p.c["ssc"])
    nc.vector.reciprocal(b["rstd"][:], b["rstd0"][:])
    spacer(nc, p)
    ins = nc.vector.tensor_scalar(y_out, r_ap, b["mneg"][:], b["rstd"][:],
                                  op0=ALU.add, op1=ALU.mult)
    ins.then_inc(p.sve, 1); p.inc("sve")


def emit_ln(p, ps0, ps1, xres, lnbuf, y_out, eps):
    nc = p.nc
    b = lnbuf
    nc.vector.tensor_tensor(b["r"][:, 0:512], ps0, xres[:, 0:512], op=ALU.add)
    ins = nc.vector.tensor_tensor(b["r"][:, 512:1024], ps1, xres[:, 512:1024], op=ALU.add)
    ins.then_inc(p.sve, 1); p.inc("sve")
    emit_ln_core(p, b["r"][:], lnbuf, y_out, eps)


def alloc_lnbuf(p, tag=""):
    return {
        "r": p.sbuf(f"ln{tag}_r", [128, 1024], F32),
        "sq": p.sbuf(f"ln{tag}_sq", [128, 1024], BF16),
        "s1": p.sbuf(f"ln{tag}_s1", [128, 1], F32),
        "ssq": p.sbuf(f"ln{tag}_ssq", [128, 1], F32),
        "mneg": p.sbuf(f"ln{tag}_mneg", [128, 1], F32),
        "msq": p.sbuf(f"ln{tag}_msq", [128, 1], F32),
        "var0": p.sbuf(f"ln{tag}_var0", [128, 1], F32),
        "var": p.sbuf(f"ln{tag}_var", [128, 1], F32),
        "rstd0": p.sbuf(f"ln{tag}_rstd0", [128, 1], F32),
        "rstd": p.sbuf(f"ln{tag}_rstd", [128, 1], F32),
    }


def alloc_tiny(p, tag=""):
    return {
        "t11": p.sbuf(f"tn{tag}_t11", [1, 1], F32),
        "t11b": p.sbuf(f"tn{tag}_t11b", [1, 1], F32),
        "t11c": p.sbuf(f"tn{tag}_t11c", [1, 1], BF16),
        "redb": p.sbuf(f"tn{tag}_redb", [128, 32], BF16),
        "acc1": p.sbuf(f"tn{tag}_acc1", [128, 1], F32),
    }


def build_l0ffn(gelu_fn=AF.Gelu):
    p = P()
    nc = p.nc
    FF = 4 * D
    NFS = FF // 128
    CH = 512
    NCH = T // CH
    NTC = CH // 128

    x_in = nc.dram_tensor("x_in", [T, D], F32, kind="ExternalInput").ap()
    w1t = nc.dram_tensor("w1t", [8, 128, FF], F32, kind="ExternalInput").ap()
    b1a = nc.dram_tensor("b1a", [128, NFS], F32, kind="ExternalInput").ap()
    w2t = nc.dram_tensor("w2t", [32, 128, D], F32, kind="ExternalInput").ap()
    b2r = nc.dram_tensor("b2r", [1, D], F32, kind="ExternalInput").ap()
    ident_h = nc.dram_tensor("ident", [128, 128], BF16, kind="ExternalInput").ap()
    ones1_h = nc.dram_tensor("ones1", [1, 128], BF16, kind="ExternalInput").ap()
    onesf_h = nc.dram_tensor("onesf", [128, 1], F32, kind="ExternalInput").ap()
    onescol_h = nc.dram_tensor("onescol", [128, 1], BF16, kind="ExternalInput").ap()
    onesrf_h = nc.dram_tensor("onesrf", [1, 128], F32, kind="ExternalInput").ap()
    x_out = nc.dram_tensor("x_out", [T, D], F32, kind="ExternalOutput").ap()
    dbg_w1 = nc.dram_tensor("dbg_w1", [128, 8, 4096], BF16, kind="ExternalOutput").ap()
    dbg_rcp = nc.dram_tensor("dbg_rcp", [2, 128, 1], F32, kind="ExternalOutput").ap()
    dbg_hT = nc.dram_tensor("dbg_hT", [128, 32, 512], BF16, kind="ExternalOutput").ap()
    dbg_xT = nc.dram_tensor("dbg_xT", [128, 8, 512], BF16, kind="ExternalOutput").ap()

    alloc_common(p)

    w1b = p.sbuf("w1b", [128, 8, FF], BF16)
    w2b = p.sbuf("w2b", [128, 32, D], BF16)
    stage = p.sbuf("stage", [128, 1024], F32)
    stage_i = p.sbuf("stage_i", [128, 1024], I32)
    red = p.sbuf("red", [128, 32], F32)
    rcp1 = p.sbuf("rcp1", [128, 1], F32)
    rcp2 = p.sbuf("rcp2", [128, 1], F32)
    b1t = p.sbuf("b1t", [128, NFS], F32)
    b2b = p.sbuf("b2b", [1, D], BF16)
    b2f = p.sbuf("b2f", [1, D], F32)
    tiny = alloc_tiny(p)
    lnbuf = alloc_lnbuf(p)
    xc = p.sbuf("xc", [128, NTC, D], F32)
    xcb = p.sbuf("xcb", [128, D], BF16)
    xT = p.sbuf("xT", [128, 8, CH], BF16)
    hT = p.sbuf("hT", [128, NFS, CH], BF16)

    ps_h = [p.psum(f"ps_h{i}", [128, 512], F32) for i in range(2)]
    ps_y = [p.psum(f"ps_y{i}", [128, 512], F32) for i in range(2)]
    ps_t = [p.psum(f"ps_t{i}", [128, 128], BF16) for i in range(2)]
    ps_s = p.psum("ps_s", [1, 32], F32)
    ps_r = p.psum("ps_r", [128, 1], F32)

    blk = p.es.enter_context(nc.Block())

    load_consts(p, ident_h, ones1_h, onesf_h, onescol_h, onesrf_h)
    nc.sync.dma_start(out=b1t[:], in_=b1a).then_inc(p.sld, 16); p.inc("sld", 16)
    nc.sync.dma_start(out=b2f[:], in_=b2r).then_inc(p.sld, 16); p.inc("sld", 16)
    nc.scalar.wait_ge(p.sld, p.c["sld"])
    ins = nc.scalar.activation(b2b[:], b2f[:], AF.Copy)
    ins.then_inc(p.ssc, 1); p.inc("ssc")

    tern_prep(p, lambda s: w1b[:, s, :], w1t, 8, FF, stage, stage_i, red, rcp1,
              tiny, ps_s, ps_r, FF * D)
    tern_prep(p, lambda s: w2b[:, s, :], w2t, 32, D, stage, stage_i, red, rcp2,
              tiny, ps_s, ps_r, FF * D)
    w_ready_sve = p.c["sve"]
    b2_ready_ssc = 1

    ps_h_free = [0, 0]
    ps_y_free = 0
    ps_t_free = [0, 0]

    for ch in range(NCH):
        if ch > 0:
            nc.sync.wait_ge(p.sst, p.c["sst"])
        src = x_in[ch * CH:(ch + 1) * CH, :].rearrange("(tl q) d -> q tl d", q=128)
        nc.sync.dma_start(out=xc[:], in_=src).then_inc(p.sld, 16)
        p.inc("sld", 16)
        x_loaded = p.c["sld"]
        nc.scalar.wait_ge(p.sld, x_loaded)
        k = 0
        for tl in range(NTC):
            ins = nc.scalar.activation(xcb[:], xc[:, tl, :], AF.Copy)
            ins.then_inc(p.ssc, 1); p.inc("ssc")
            nc.tensor.wait_ge(p.ssc, p.c["ssc"])
            for ds in range(8):
                pt = ps_t[k % 2]
                if ps_t_free[k % 2] > 0:
                    nc.tensor.wait_ge(p.ssc, ps_t_free[k % 2])
                ins = nc.tensor.transpose(pt[:], xcb[:, ds * 128:(ds + 1) * 128],
                                          p.idt[:])
                ins.then_inc(p.smm, 1); p.inc("smm")
                nc.scalar.wait_ge(p.smm, p.c["smm"])
                ins = nc.scalar.activation(xT[:, ds, tl * 128:(tl + 1) * 128], pt[:],
                                           AF.Copy)
                ins.then_inc(p.ssc, 1); p.inc("ssc")
                ps_t_free[k % 2] = p.c["ssc"]
                k += 1
        xT_done = p.c["ssc"]
        nc.tensor.wait_ge(p.ssc, xT_done)
        if ch == 0:
            nc.tensor.wait_ge(p.sve, w_ready_sve)
        for fs in range(NFS):
            for tc2 in range(CH // 512):
                pid = fs % 2
                if ps_h_free[pid] > 0:
                    nc.tensor.wait_ge(p.ssc, ps_h_free[pid])
                for s in range(8):
                    ins = nc.tensor.matmul(
                        ps_h[pid][:], w1b[:, s, fs * 128:(fs + 1) * 128],
                        xT[:, s, tc2 * 512:(tc2 + 1) * 512],
                        start=(s == 0), stop=(s == 7))
                ins.then_inc(p.smm, 1); p.inc("smm")
                nc.scalar.wait_ge(p.smm, p.c["smm"])
                ins = nc.scalar.activation(hT[:, fs, tc2 * 512:(tc2 + 1) * 512],
                                           ps_h[pid][:], gelu_fn,
                                           bias=b1t[:, fs:fs + 1], scale=1.0)
                ins.then_inc(p.ssc, 1); p.inc("ssc")
                ps_h_free[pid] = p.c["ssc"]
        hT_done = p.c["ssc"]
        nc.tensor.wait_ge(p.ssc, hT_done)
        for tl in range(NTC):
            if ps_y_free > 0:
                nc.tensor.wait_ge(p.sve, ps_y_free)
            for dc in range(2):
                for fs in range(NFS):
                    nc.tensor.matmul(ps_y[dc][:], hT[:, fs, tl * 128:(tl + 1) * 128],
                                     w2b[:, fs, dc * 512:(dc + 1) * 512],
                                     start=(fs == 0), stop=False)
                ins = nc.tensor.matmul(ps_y[dc][:], p.ones1[:],
                                       b2b[0:1, dc * 512:(dc + 1) * 512],
                                       start=False, stop=True)
            ins.then_inc(p.smm, 1); p.inc("smm")
            nc.vector.wait_ge(p.smm, p.c["smm"])
            emit_ln(p, ps_y[0][:], ps_y[1][:], xc[:, tl, :], lnbuf,
                    xc[:, tl, :], 1e-5)
            ps_y_free = p.c["sve"] - 2
        ln_done = p.c["sve"]
        nc.gpsimd.wait_ge(p.sve, ln_done)
        dst = x_out[ch * CH:(ch + 1) * CH, :].rearrange("(tl q) d -> q tl d", q=128)
        nc.gpsimd.dma_start(out=dst, in_=xc[:]).then_inc(p.sst, 16)
        p.inc("sst", 16)

    nc.gpsimd.dma_start(out=dbg_w1, in_=w1b[:]).then_inc(p.sst, 16); p.inc("sst", 16)
    nc.gpsimd.dma_start(out=dbg_rcp[0], in_=rcp1[:]).then_inc(p.sst, 16); p.inc("sst", 16)
    nc.gpsimd.dma_start(out=dbg_rcp[1], in_=rcp2[:]).then_inc(p.sst, 16); p.inc("sst", 16)
    nc.gpsimd.dma_start(out=dbg_hT, in_=hT[:]).then_inc(p.sst, 16); p.inc("sst", 16)
    nc.gpsimd.dma_start(out=dbg_xT, in_=xT[:]).then_inc(p.sst, 16); p.inc("sst", 16)
    nc.sync.wait_ge(p.sst, p.c["sst"])
    return p


def build_l1ffn():
    p = P()
    nc = p.nc
    FH = 3072
    NFS = FH // 128
    CH = 512
    NCH = T // CH
    NTC = CH // 128

    x_in = nc.dram_tensor("x_in", [T, D], F32, kind="ExternalInput").ap()
    wgT = nc.dram_tensor("wgT", [8, 128, FH], I32, kind="ExternalInput").ap()
    wuT = nc.dram_tensor("wuT", [8, 128, FH], I32, kind="ExternalInput").ap()
    wdT = nc.dram_tensor("wdT", [24, 128, D], I32, kind="ExternalInput").ap()
    ident_h = nc.dram_tensor("ident", [128, 128], BF16, kind="ExternalInput").ap()
    ones1_h = nc.dram_tensor("ones1", [1, 128], BF16, kind="ExternalInput").ap()
    onesf_h = nc.dram_tensor("onesf", [128, 1], F32, kind="ExternalInput").ap()
    onescol_h = nc.dram_tensor("onescol", [128, 1], BF16, kind="ExternalInput").ap()
    onesrf_h = nc.dram_tensor("onesrf", [1, 128], F32, kind="ExternalInput").ap()
    x_out = nc.dram_tensor("x_out", [T, D], F32, kind="ExternalOutput").ap()

    alloc_common(p)

    wgB = p.sbuf("wgB", [128, 8, FH], BF16)
    wuB = p.sbuf("wuB", [128, 8, FH], BF16)
    wdB = p.sbuf("wdB", [128, 24, D], BF16)
    stage = p.sbuf("stage", [128, 1024], I32)
    lnbuf = alloc_lnbuf(p)
    xc = p.sbuf("xc", [128, NTC, D], F32)
    xcb = p.sbuf("xcb", [128, D], BF16)
    xT = p.sbuf("xT", [128, 8, CH], BF16)
    hT = p.sbuf("hT", [128, NFS, CH], BF16)
    gT = p.sbuf("gT", [128, CH], BF16)
    uT = p.sbuf("uT", [128, CH], BF16)

    ps_g = p.psum("ps_g", [128, 512], F32)
    ps_u = p.psum("ps_u", [128, 512], F32)
    ps_y = [p.psum(f"ps_y{i}", [128, 512], F32) for i in range(2)]
    ps_t = [p.psum(f"ps_t{i}", [128, 128], BF16) for i in range(2)]

    blk = p.es.enter_context(nc.Block())
    load_consts(p, ident_h, ones1_h, onesf_h, onescol_h, onesrf_h)

    def dequant(dst3, src3, nsl, FREE):
        CW = 1024
        ncw = FREE // CW
        for s in range(nsl):
            for cw in range(ncw):
                nc.sync.wait_ge(p.sve, p.c["sve"])
                nc.sync.dma_start(out=stage[:], in_=src3[s][:, cw * CW:(cw + 1) * CW])\
                    .then_inc(p.sld, 16)
                p.inc("sld", 16)
                nc.vector.wait_ge(p.sld, p.c["sld"])
                ins = nc.vector.tensor_scalar_mul(dst3[:, s, cw * CW:(cw + 1) * CW],
                                                  stage[:], 1.0 / 127.0)
                ins.then_inc(p.sve, 1); p.inc("sve")

    dequant(wgB, wgT, 8, FH)
    dequant(wuB, wuT, 8, FH)
    dequant(wdB, wdT, 24, D)
    w_ready_sve = p.c["sve"]

    ps_g_free = 0
    ps_u_free = 0
    ps_y_free = 0
    ps_t_free = [0, 0]

    for ch in range(NCH):
        if ch > 0:
            nc.sync.wait_ge(p.sst, p.c["sst"])
        src = x_in[ch * CH:(ch + 1) * CH, :].rearrange("(tl q) d -> q tl d", q=128)
        nc.sync.dma_start(out=xc[:], in_=src).then_inc(p.sld, 16)
        p.inc("sld", 16)
        x_loaded = p.c["sld"]
        nc.scalar.wait_ge(p.sld, x_loaded)
        k = 0
        for tl in range(NTC):
            ins = nc.scalar.activation(xcb[:], xc[:, tl, :], AF.Copy)
            ins.then_inc(p.ssc, 1); p.inc("ssc")
            nc.tensor.wait_ge(p.ssc, p.c["ssc"])
            for ds in range(8):
                pt = ps_t[k % 2]
                if ps_t_free[k % 2] > 0:
                    nc.tensor.wait_ge(p.ssc, ps_t_free[k % 2])
                ins = nc.tensor.transpose(pt[:], xcb[:, ds * 128:(ds + 1) * 128],
                                          p.idt[:])
                ins.then_inc(p.smm, 1); p.inc("smm")
                nc.scalar.wait_ge(p.smm, p.c["smm"])
                ins = nc.scalar.activation(xT[:, ds, tl * 128:(tl + 1) * 128], pt[:],
                                           AF.Copy)
                ins.then_inc(p.ssc, 1); p.inc("ssc")
                ps_t_free[k % 2] = p.c["ssc"]
                k += 1
        xT_done = p.c["ssc"]
        nc.tensor.wait_ge(p.ssc, xT_done)
        if ch == 0:
            nc.tensor.wait_ge(p.sve, w_ready_sve)
        for fs in range(NFS):
            if ps_g_free > 0:
                nc.tensor.wait_ge(p.sve, ps_g_free)
            for s in range(8):
                nc.tensor.matmul(ps_g[:], wgB[:, s, fs * 128:(fs + 1) * 128],
                                 xT[:, s, :], start=(s == 0), stop=(s == 7))
            for s in range(8):
                ins = nc.tensor.matmul(ps_u[:], wuB[:, s, fs * 128:(fs + 1) * 128],
                                       xT[:, s, :], start=(s == 0), stop=(s == 7))
            ins.then_inc(p.smm, 1); p.inc("smm")
            nc.scalar.wait_ge(p.smm, p.c["smm"])
            nc.scalar.activation(gT[:], ps_g[:], AF.Silu)
            ins = nc.scalar.activation(uT[:], ps_u[:], AF.Copy)
            ins.then_inc(p.ssc, 1); p.inc("ssc")
            nc.vector.wait_ge(p.ssc, p.c["ssc"])
            ins = nc.vector.tensor_tensor(hT[:, fs, :], gT[:], uT[:], op=ALU.mult)
            ins.then_inc(p.sve, 1); p.inc("sve")
            ps_g_free = p.c["sve"]
        hT_done = p.c["sve"]
        nc.tensor.wait_ge(p.sve, hT_done)
        for tl in range(NTC):
            if ps_y_free > 0:
                nc.tensor.wait_ge(p.sve, ps_y_free)
            for dc in range(2):
                for fs in range(NFS):
                    ins = nc.tensor.matmul(ps_y[dc][:],
                                           hT[:, fs, tl * 128:(tl + 1) * 128],
                                           wdB[:, fs, dc * 512:(dc + 1) * 512],
                                           start=(fs == 0), stop=(fs == NFS - 1))
            ins.then_inc(p.smm, 1); p.inc("smm")
            nc.vector.wait_ge(p.smm, p.c["smm"])
            emit_ln(p, ps_y[0][:], ps_y[1][:], xc[:, tl, :], lnbuf,
                    xc[:, tl, :], 1e-5)
            ps_y_free = p.c["sve"] - 2
        ln_done = p.c["sve"]
        nc.gpsimd.wait_ge(p.sve, ln_done)
        dst = x_out[ch * CH:(ch + 1) * CH, :].rearrange("(tl q) d -> q tl d", q=128)
        nc.gpsimd.dma_start(out=dst, in_=xc[:]).then_inc(p.sst, 16)
        p.inc("sst", 16)

    nc.sync.wait_ge(p.sst, p.c["sst"])
    return p



DBG_TILE = 0

def build_l0attn(dbg=False):
    p = P()
    nc = p.nc
    x_in = nc.dram_tensor("x_in", [T, D], F32, kind="ExternalInput").ap()
    tri_h = nc.dram_tensor("tri", [128, 128], BF16, kind="ExternalInput").ap()
    wpe_h = nc.dram_tensor("wpe", [128, 124], F32, kind="ExternalInput").ap()
    bpe_h = nc.dram_tensor("bpe", [1, 1024], F32, kind="ExternalInput").ap()
    stt_h = nc.dram_tensor("stt", [1, 496], F32, kind="ExternalInput").ap()
    recip_h = nc.dram_tensor("recip", [128, 32], F32, kind="ExternalInput").ap()
    mask0_h = nc.dram_tensor("mask0", [128, 496], F32, kind="ExternalInput").ap()
    wfcT_h = nc.dram_tensor("wfcT", [8, 128, D], F32, kind="ExternalInput").ap()
    gam_h = nc.dram_tensor("gam", [1, D], F32, kind="ExternalInput").ap()
    bet_h = nc.dram_tensor("bet", [1, D], F32, kind="ExternalInput").ap()
    ident_h = nc.dram_tensor("ident", [128, 128], BF16, kind="ExternalInput").ap()
    ones1_h = nc.dram_tensor("ones1", [1, 128], BF16, kind="ExternalInput").ap()
    onesf_h = nc.dram_tensor("onesf", [128, 1], F32, kind="ExternalInput").ap()
    onescol_h = nc.dram_tensor("onescol", [128, 1], BF16, kind="ExternalInput").ap()
    onesrf_h = nc.dram_tensor("onesrf", [1, 128], F32, kind="ExternalInput").ap()
    ibig_h = nc.dram_tensor("ibig", [128, 260], BF16, kind="ExternalInput").ap()
    jbig_h = nc.dram_tensor("jbig", [30, 286], BF16, kind="ExternalInput").ap()
    x_out = nc.dram_tensor("x_out", [T, D], F32, kind="ExternalOutput").ap()
    if dbg:
        dbg_ci = nc.dram_tensor("dbg_ci", [128, D], F32, kind="ExternalOutput").ap()
        dbg_en = nc.dram_tensor("dbg_en", [128, 496], F32, kind="ExternalOutput").ap()
        dbg_w = nc.dram_tensor("dbg_w", [128, 496], F32, kind="ExternalOutput").ap()
        dbg_cv = nc.dram_tensor("dbg_cv", [128, D], F32, kind="ExternalOutput").ap()
        dbg_vr = nc.dram_tensor("dbg_vr", [128, 1024], BF16, kind="ExternalOutput").ap()
        dbg_wfcb = nc.dram_tensor("dbg_wfcb", [128, 8192], BF16, kind="ExternalOutput").ap()
        dbg_xh = nc.dram_tensor("dbg_xh", [128, D], F32, kind="ExternalOutput").ap()

    alloc_common(p)
    p.sem("stl")

    lnbuf = alloc_lnbuf(p)
    xall = p.sbuf("xall", [128, NT, D], BF16)
    ci_cur = p.sbuf("ci_cur", [128, D], BF16)
    ci_tail = p.sbuf("ci_tail", [30, D], BF16)
    ciT = p.sbuf("ciT", [128, 8, 128], BF16)
    carry = p.sbuf("carry", [1, D], BF16)
    wpeB = p.sbuf("wpeB", [128, 124], BF16)
    bpeB = p.sbuf("bpeB", [1, 1024], BF16)
    gamr = p.sbuf("gamr", [1, D], F32)
    betr = p.sbuf("betr", [1, D], F32)
    bper = p.sbuf("bper", [1, 1024], F32)
    sttr = p.sbuf("sttr", [1, 496], F32)
    sttb = p.sbuf("sttb", [128, 496], F32)
    maskb = p.sbuf("maskb", [128, 496], F32)
    recipb = p.sbuf("recipb", [128, 32], F32)
    gamb = p.sbuf("gamb", [128, D], F32)
    betb = p.sbuf("betb", [128, D], F32)
    wfcB = p.sbuf("wfcB", [128, 8, D], BF16)
    sg = p.sbuf("sg", [128, 496], F32)
    en = p.sbuf("en", [128, 496], F32)
    enb = p.sbuf("enb", [128, 496], F32)
    mx = p.sbuf("mx", [128, 16], F32)
    ssum = p.sbuf("ssum", [128, 16], F32)
    rs = p.sbuf("rs", [128, 16], F32)
    w_bf = p.sbuf("w_bf", [128, 496], BF16)
    WS = p.sbuf("WS", [128, 31, 16], BF16)
    WSc = p.sbuf("WSc", [30, 31, 16], BF16)
    rhsring = p.sbuf("rhsring", [128, 4, D], BF16)
    rhsB = p.sbuf("rhsB", [30, 2, D], BF16)
    ibig = p.sbuf("ibigs", [128, 260], BF16)
    jbig = p.sbuf("jbigs", [30, 286], BF16)
    cv_bf = p.sbuf("cv_bf", [128, D], BF16)
    dbg_tmp = p.sbuf("dbg_tmp", [128, D], F32) if True else None
    trib = p.sbuf("trib", [128, 128], BF16)
    xhat = p.sbuf("xhat", [128, D], F32)
    bx = p.sbuf("bx", [128, D], F32)
    r2 = xhat
    yout = bx

    ps_big = [p.psum(f"ps_big{i}", [128, 512], F32) for i in range(2)]
    ps_t = [p.psum(f"ps_t{i}", [128, 128], BF16) for i in range(2)]
    ps_c = p.psum("ps_c", [1, 1024], F32)

    stage = lnbuf["r"]
    blk = p.es.enter_context(nc.Block())
    load_consts(p, ident_h, ones1_h, onesf_h, onescol_h, onesrf_h)
    for dst, srch in [(recipb[:], recip_h), (maskb[:], mask0_h), (gamr[:], gam_h),
                      (betr[:], bet_h), (bper[:], bpe_h), (sttr[:], stt_h),
                      (trib[:], tri_h), (stage[:, 0:124], wpe_h),
                      (ibig[:], ibig_h), (jbig[:], jbig_h)]:
        nc.sync.dma_start(out=dst, in_=srch).then_inc(p.sld, 16)
        p.inc("sld", 16)
    x_load_pending = True
    all_loaded = None
    nc.scalar.wait_ge(p.sld, p.c["sld"])
    nc.scalar.activation(wpeB[:], stage[:, 0:124], AF.Copy)
    ins = nc.scalar.activation(bpeB[:], bper[:], AF.Copy)
    ins.then_inc(p.ssc, 1); p.inc("ssc")
    def bc_full(dst, row):
        nc.tensor.wait_ge(p.sld, p.c["sld"])
        if p.c["sve"] > 0:
            nc.tensor.wait_ge(p.sve, p.c["sve"])
        n = dst.shape[1]
        mmi = nc.tensor.matmul(ps_big[0][:, 0:min(512, n)], p.onesrf[:],
                               row[0:1, 0:min(512, n)], start=True, stop=True)
        if n > 512:
            mmi = nc.tensor.matmul(ps_big[1][:, 0:n - 512], p.onesrf[:],
                                   row[0:1, 512:n], start=True, stop=True)
        mmi.then_inc(p.smm, 1); p.inc("smm")
        nc.vector.wait_ge(p.smm, p.c["smm"])
        nc.vector.tensor_copy(dst[:, 0:min(512, n)], ps_big[0][:, 0:min(512, n)])
        if n > 512:
            ins = nc.vector.tensor_copy(dst[:, 512:n], ps_big[1][:, 0:n - 512])
        else:
            ins = nc.vector.tensor_copy(p.j1[:], p.onesf[:])
        ins.then_inc(p.sve, 1); p.inc("sve")
    bc_full(sttb, sttr)
    bc_full(gamb, gamr)
    bc_full(betb, betr)
    for s in range(8):
        nc.sync.wait_ge(p.sve, p.c["sve"])
        nc.sync.dma_start(out=stage[:], in_=wfcT_h[s]).then_inc(p.sld, 16)
        p.inc("sld", 16)
        nc.vector.wait_ge(p.sld, p.c["sld"])
        ins = nc.vector.tensor_copy(wfcB[:, s, :], stage[:])
        ins.then_inc(p.sve, 1); p.inc("sve")
    nc.vector.memset(WS[:], 0.0)
    nc.vector.memset(WSc[:], 0.0)
    nc.vector.memset(rhsring[:], 0.0)
    ins = nc.vector.memset(rhsB[:], 0.0)
    ins.then_inc(p.sve, 1); p.inc("sve")
    nc.gpsimd.wait_ge(p.sve, p.c["sve"])
    nc.sync.wait_ge(p.sve, p.c["sve"])
    for tl in range(NT):
        nc.sync.wait_ge(p.ssc, p.c["ssc"])
        nc.sync.dma_start(out=stage[:], in_=x_in[tl * 128:(tl + 1) * 128, :])\
            .then_inc(p.sld, 16)
        p.inc("sld", 16)
        nc.scalar.wait_ge(p.sld, p.c["sld"])
        ins = nc.scalar.activation(xall[:, tl, :], stage[:], AF.Copy)
        ins.then_inc(p.ssc, 1); p.inc("ssc")
    all_loaded = p.c["ssc"]

    ps_t_free = [0, 0]

    for i in range(NT):
        xtb = xall[:, i, :]
        if i == 0:
            nc.tensor.wait_ge(p.ssc, all_loaded)
            nc.tensor.wait_ge(p.sve, p.c["sve"])
        else:
            nc.tensor.wait_ge(p.sve, p.c["sve"])
        for dc in range(2):
            nc.tensor.matmul(ps_big[dc][:], trib[:], xtb[:, dc * 512:(dc + 1) * 512],
                             start=True, stop=(i == 0))
            if i > 0:
                nc.tensor.matmul(ps_big[dc][:], p.ones1[:],
                                 carry[0:1, dc * 512:(dc + 1) * 512],
                                 start=False, stop=True)
        for dc in range(2):
            ins = nc.tensor.matmul(ps_c[0:1, dc * 512:(dc + 1) * 512], p.onescol[:],
                                   xtb[:, dc * 512:(dc + 1) * 512],
                                   start=(i == 0), stop=(i == NT - 1))
        ins.then_inc(p.smm, 1); p.inc("smm")
        nc.scalar.wait_ge(p.smm, p.c["smm"])
        nc.scalar.activation(ci_cur[:, 0:512], ps_big[0][:], AF.Identity,
                             scale=recipb[:, i:i + 1], bias=0.0)
        nc.scalar.activation(ci_cur[:, 512:1024], ps_big[1][:], AF.Identity,
                             scale=recipb[:, i:i + 1], bias=0.0)
        ins = nc.scalar.activation(carry[:], ps_c[0:1, :], AF.Copy)
        ins.then_inc(p.ssc, 1); p.inc("ssc")
        ci_ssc = p.c["ssc"]
        nc.tensor.wait_ge(p.ssc, ci_ssc)
        for ds in range(8):
            pt = ps_t[ds % 2]
            if ps_t_free[ds % 2] > 0:
                nc.tensor.wait_ge(p.ssc, ps_t_free[ds % 2])
            ins = nc.tensor.transpose(pt[:], ci_cur[:, ds * 128:(ds + 1) * 128], p.idt[:])
            ins.then_inc(p.smm, 1); p.inc("smm")
            nc.scalar.wait_ge(p.smm, p.c["smm"])
            ins = nc.scalar.activation(ciT[:, ds, :], pt[:], AF.Copy)
            ins.then_inc(p.ssc, 1); p.inc("ssc")
            ps_t_free[ds % 2] = p.c["ssc"]
        nc.tensor.wait_ge(p.ssc, p.c["ssc"])
        for s in range(8):
            pb = ps_big[s // 4]
            c0 = (s % 4) * 128
            nc.tensor.matmul(pb[:, c0:c0 + 124], ciT[:, s, :], wpeB[:],
                             start=True, stop=False)
            ins = nc.tensor.matmul(pb[:, c0:c0 + 124], p.ones1[:],
                                   bpeB[0:1, s * 128:s * 128 + 124],
                                   start=False, stop=True)
        ins.then_inc(p.smm, 1); p.inc("smm")
        nc.scalar.wait_ge(p.smm, p.c["smm"])
        sg3a = sg[:].rearrange("p (k h) -> p h k", h=16)
        for half in range(2):
            pb3 = ps_big[half][:].rearrange("p (s r) -> p s r", r=128)
            for hp in range(2):
                g0 = 8 * half + hp
                ins = nc.scalar.activation(
                    sg3a[:, g0:min(g0 + 8, 16):2, :].rearrange("p b k -> p k b")
                    if False else sg3a[:, g0:min(g0 + 8, 16):2, :],
                    pb3[:, :, 31 + hp * 62:62 + hp * 62],
                    AF.Sigmoid)
        ins.then_inc(p.ssc, 1); p.inc("ssc")
        nc.vector.wait_ge(p.ssc, p.c["ssc"])
        en3 = en[:].rearrange("p (k h) -> p h k", h=16)
        for half in range(2):
            pb3 = ps_big[half][:].rearrange("p (s r) -> p s r", r=128)
            for hp in range(2):
                g0 = 8 * half + hp
                nc.vector.tensor_tensor(
                    en3[:, g0:min(g0 + 8, 16):2, :],
                    sg3a[:, g0:min(g0 + 8, 16):2, :],
                    pb3[:, :, hp * 62:hp * 62 + 31], op=ALU.mult)
        en_sve_raw = p.c["sve"]
        spacer(nc, p)
        ins = nc.vector.tensor_tensor(enb[:], en[:], sttb[:], op=ALU.add)
        ins.then_inc(p.sve, 1); p.inc("sve")
        psbig_free_sve = p.c["sve"]
        enb_hk = enb[:].rearrange("p (k h) -> p h k", h=16)
        en_hk = en[:].rearrange("p (k h) -> p h k", h=16)
        nc.vector.tensor_reduce(mx[:].unsqueeze(-1), enb_hk, axis=AX.X, op=ALU.max)
        spacer(nc, p)
        mxb = mx[:].unsqueeze(-1).broadcast_to([128, 16, 31])
        ins = nc.vector.tensor_tensor(en_hk, enb_hk, mxb, op=ALU.subtract)
        ins.then_inc(p.sve, 1); p.inc("sve")
        nc.scalar.wait_ge(p.sve, p.c["sve"])
        ins = nc.scalar.activation(enb[:], en[:], AF.Exp)
        ins.then_inc(p.ssc, 1); p.inc("ssc")
        nc.vector.wait_ge(p.ssc, p.c["ssc"])
        nc.vector.tensor_reduce(ssum[:].unsqueeze(-1), enb_hk, axis=AX.X, op=ALU.add)
        spacer(nc, p)
        nc.vector.reciprocal(rs[:], ssum[:])
        spacer(nc, p)
        rsb = rs[:].unsqueeze(-1).broadcast_to([128, 16, 31])
        nc.vector.tensor_tensor(en_hk, enb_hk, rsb, op=ALU.mult)
        if i == 0:
            spacer(nc, p)
            nc.vector.tensor_tensor(en[:], en[:], maskb[:], op=ALU.mult)
        else:
            spacer(nc, p)
        ins = nc.vector.tensor_copy(w_bf[:], en[:])
        ins.then_inc(p.sve, 1); p.inc("sve")
        wbf_sve = p.c["sve"]
        if dbg and i == DBG_TILE:
            nc.gpsimd.wait_ge(p.sve, wbf_sve)
            nc.gpsimd.dma_start(out=dbg_ci, in_=ci_cur[:]).then_inc(p.sst, 16); p.inc("sst", 16)
            nc.gpsimd.dma_start(out=dbg_en, in_=enb[:]).then_inc(p.sst, 16); p.inc("sst", 16)
            nc.gpsimd.dma_start(out=dbg_w, in_=en[:]).then_inc(p.sst, 16); p.inc("sst", 16)
        nc.gpsimd.wait_ge(p.sve, wbf_sve)
        for d_ in range(31):
            nc.gpsimd.dma_start(out=WS[0:128 - d_, d_, :],
                                in_=w_bf[d_:128, (30 - d_) * 16:(31 - d_) * 16])\
                .then_inc(p.stl, 16)
            p.inc("stl", 16)
        if i > 0:
            for d_ in range(1, 31):
                nc.gpsimd.dma_start(out=WSc[30 - d_:30, d_, :],
                                    in_=w_bf[0:d_, (30 - d_) * 16:(31 - d_) * 16])\
                    .then_inc(p.stl, 16)
                p.inc("stl", 16)
        ws_done = p.c["stl"]
        nc.vector.wait_ge(p.stl, ws_done)
        nc.vector.wait_ge(p.ssc, p.c["ssc"])
        ci_hd = ci_cur[:].rearrange("p (h e) -> p h e", h=16)
        first_mm = True
        ring_free = [0, 0, 0, 0]
        for d_ in range(31):
            rb = rhsring[:, d_ % 4, :]
            if ring_free[d_ % 4] > 0:
                nc.vector.wait_ge(p.smm, ring_free[d_ % 4])
            wsb = WS[:, d_, :].unsqueeze(-1).broadcast_to([128, 16, 64])
            ins = nc.vector.tensor_tensor(rb.rearrange("p (h e) -> p h e", h=16),
                                          ci_hd, wsb, op=ALU.mult)
            ins.then_inc(p.sve, 1); p.inc("sve")
            nc.tensor.wait_ge(p.sve, p.c["sve"])
            lhs = ibig[:, 128 - d_:256 - d_]
            for dc in range(2):
                ins = nc.tensor.matmul(ps_big[dc][:], lhs,
                                       rb[dc * 512:(dc + 1) * 512]
                                       if False else rb.rearrange("p f -> p f")[:, dc * 512:(dc + 1) * 512],
                                       start=first_mm, stop=False)
            first_mm = False
            ins.then_inc(p.smm, 1); p.inc("smm")
            ring_free[d_ % 4] = p.c["smm"]
        if i > 0:
            ci_t_hd = ci_tail[:].rearrange("p (h e) -> p h e", h=16)
            for d_ in range(1, 31):
                rb = rhsB[:, d_ % 2, :]
                if d_ > 2:
                    nc.vector.wait_ge(p.smm, p.c["smm"] - 1)
                wsb = WSc[:, d_, :].unsqueeze(-1).broadcast_to([30, 16, 64])
                ins = nc.vector.tensor_tensor(rb.rearrange("p (h e) -> p h e", h=16),
                                              ci_t_hd, wsb, op=ALU.mult)
                ins.then_inc(p.sve, 1); p.inc("sve")
                nc.tensor.wait_ge(p.sve, p.c["sve"])
                lhs = jbig[:, 158 - d_:286 - d_]
                for dc in range(2):
                    ins = nc.tensor.matmul(ps_big[dc][:], lhs,
                                           rb[:, dc * 512:(dc + 1) * 512],
                                           start=False, stop=False)
                ins.then_inc(p.smm, 1); p.inc("smm")
        ins = nc.tensor.matmul(ps_big[0][:], ibig[:, 0:128], rhsring[:, 0, 0:512],
                               start=False, stop=True)
        ins = nc.tensor.matmul(ps_big[1][:], ibig[:, 0:128], rhsring[:, 0, 512:1024],
                               start=False, stop=True)
        ins.then_inc(p.smm, 1); p.inc("smm")
        win_smm = p.c["smm"]
        nc.gpsimd.wait_ge(p.smm, win_smm)
        nc.gpsimd.dma_start(out=ci_tail[:], in_=ci_cur[98:128, :]).then_inc(p.stl, 16)
        p.inc("stl", 16)
        if dbg and i == DBG_TILE:
            nc.gpsimd.wait_ge(p.ssc, p.c["ssc"] + 2)
        nc.scalar.wait_ge(p.smm, win_smm)
        nc.scalar.activation(cv_bf[:, 0:512], ps_big[0][:], AF.Copy)
        ins = nc.scalar.activation(cv_bf[:, 512:1024], ps_big[1][:], AF.Copy)
        ins.then_inc(p.ssc, 1); p.inc("ssc")
        if dbg and i == DBG_TILE:
            nc.gpsimd.dma_start(out=dbg_cv, in_=cv_bf[:]).then_inc(p.sst, 16); p.inc("sst", 16)
        nc.tensor.wait_ge(p.ssc, p.c["ssc"])
        for ds in range(8):
            pt = ps_t[ds % 2]
            nc.tensor.wait_ge(p.ssc, ps_t_free[ds % 2])
            ins = nc.tensor.transpose(pt[:], cv_bf[:, ds * 128:(ds + 1) * 128], p.idt[:])
            ins.then_inc(p.smm, 1); p.inc("smm")
            nc.scalar.wait_ge(p.smm, p.c["smm"])
            ins = nc.scalar.activation(ciT[:, ds, :], pt[:], AF.Copy)
            ins.then_inc(p.ssc, 1); p.inc("ssc")
            ps_t_free[ds % 2] = p.c["ssc"]
        nc.tensor.wait_ge(p.ssc, p.c["ssc"])
        for dc in range(2):
            for s in range(8):
                ins = nc.tensor.matmul(ps_big[dc][:], ciT[:, s, :],
                                       wfcB[:, s, dc * 512:(dc + 1) * 512],
                                       start=(s == 0), stop=(s == 7))
        ins.then_inc(p.smm, 1); p.inc("smm")
        nc.vector.wait_ge(p.smm, p.c["smm"])
        nc.vector.tensor_tensor(bx[:], betb[:], xall[:, i, :], op=ALU.add)
        emit_ln(p, ps_big[0][:], ps_big[1][:], xall[:, i, :], lnbuf, xhat[:], 1e-6)
        if dbg and i == DBG_TILE:
            nc.vector.tensor_copy(dbg_tmp[:, 0:512], ps_big[0][:])
            ins = nc.vector.tensor_copy(dbg_tmp[:, 512:1024], ps_big[1][:])
            ins.then_inc(p.sve, 1); p.inc("sve")
            nc.gpsimd.wait_ge(p.sve, p.c["sve"])
            nc.gpsimd.dma_start(out=dbg_vr, in_=ciT[:].rearrange("p a b -> p (a b)")).then_inc(p.sst, 16); p.inc("sst", 16)
            nc.gpsimd.dma_start(out=dbg_xh, in_=dbg_tmp[:]).then_inc(p.sst, 16); p.inc("sst", 16)
            nc.gpsimd.dma_start(out=dbg_wfcb, in_=wfcB[:].rearrange("p a b -> p (a b)")).then_inc(p.sst, 16); p.inc("sst", 16)
            nc.vector.wait_ge(p.sst, p.c["sst"])
        spacer(nc, p)
        nc.vector.scalar_tensor_tensor(lnbuf["r"][:], xhat[:], 1.0, gamb[:],
                                       op0=ALU.mult, op1=ALU.mult)
        spacer(nc, p)
        ins = nc.vector.tensor_tensor(r2[:], lnbuf["r"][:], bx[:], op=ALU.add)
        ins.then_inc(p.sve, 1); p.inc("sve")
        emit_ln_core(p, r2[:], lnbuf, yout[:], 1e-5)
        nc.gpsimd.wait_ge(p.sve, p.c["sve"])
        dst = x_out[i * 128:(i + 1) * 128, :]
        nc.gpsimd.dma_start(out=dst, in_=yout[:]).then_inc(p.sst, 16)
        p.inc("sst", 16)

    nc.sync.wait_ge(p.sst, p.c["sst"])
    return p


def build_l1attn(dbg=False):
    p = P()
    nc = p.nc
    PC = 512
    NPC = T // PC
    SC = 128
    NSC = PC // SC

    x_in = nc.dram_tensor("x_in", [T, D], F32, kind="ExternalInput").ap()
    wqT_h = nc.dram_tensor("wqT", [8, 128, D], F32, kind="ExternalInput").ap()
    wkT_h = nc.dram_tensor("wkT", [8, 128, D], F32, kind="ExternalInput").ap()
    wvT_h = nc.dram_tensor("wvT", [8, 128, D], F32, kind="ExternalInput").ap()
    wgT_h = nc.dram_tensor("wgT", [8, 128, D], F32, kind="ExternalInput").ap()
    woT_h = nc.dram_tensor("woT", [8, 128, D], F32, kind="ExternalInput").ap()
    waT_h = nc.dram_tensor("waT", [8, 128, 128], F32, kind="ExternalInput").ap()
    wbT_h = nc.dram_tensor("wbT", [8, 128, 128], F32, kind="ExternalInput").ap()
    cw_h = nc.dram_tensor("cw", [12, 1, D], F32, kind="ExternalInput").ap()
    cbrow_h = nc.dram_tensor("cbrow", [2, D], F32, kind="ExternalInput").ap()
    cb_h = nc.dram_tensor("cb", [128, 8, 4], F32, kind="ExternalInput").ap()
    barep_h = nc.dram_tensor("barep", [128, 2], F32, kind="ExternalInput").ap()
    oneblk2_h = nc.dram_tensor("oneblk2", [128, 2], BF16, kind="ExternalInput").ap()
    repsel_h = nc.dram_tensor("repsel", [16, 128], BF16, kind="ExternalInput").ap()
    ident_h = nc.dram_tensor("ident", [128, 128], BF16, kind="ExternalInput").ap()
    ones1_h = nc.dram_tensor("ones1", [1, 128], BF16, kind="ExternalInput").ap()
    onesf_h = nc.dram_tensor("onesf", [128, 1], F32, kind="ExternalInput").ap()
    onescol_h = nc.dram_tensor("onescol", [128, 1], BF16, kind="ExternalInput").ap()
    onesrf_h = nc.dram_tensor("onesrf", [1, 128], F32, kind="ExternalInput").ap()
    x_out = nc.dram_tensor("x_out", [T, D], F32, kind="ExternalOutput").ap()
    wq_s = nc.dram_tensor("wq_s", [4, 8, 128, D], BF16, kind="Internal").ap()
    wk_s = nc.dram_tensor("wk_s", [4, 8, 128, D], BF16, kind="Internal").ap()
    wv_s = nc.dram_tensor("wv_s", [4, 8, 128, D], BF16, kind="Internal").ap()
    wg_s = nc.dram_tensor("wg_s", [8, 128, D], BF16, kind="Internal").ap()
    wo_s = nc.dram_tensor("wo_s", [8, 128, D], BF16, kind="Internal").ap()
    NSCR = 16 * 64 * SC
    kscr = nc.dram_tensor("kscr", [NSCR], BF16, kind="Internal").ap()
    qscr = nc.dram_tensor("qscr", [NSCR], BF16, kind="Internal").ap()
    vscr = nc.dram_tensor("vscr", [NSCR], BF16, kind="Internal").ap()
    oscr = nc.dram_tensor("oscr", [NSCR], BF16, kind="Internal").ap()
    if dbg:
        dbg_q = nc.dram_tensor("dbg_q", [128, 8, PC], BF16, kind="ExternalOutput").ap()
        dbg_k = nc.dram_tensor("dbg_k", [128, 8, PC], BF16, kind="ExternalOutput").ap()
        dbg_ab = nc.dram_tensor("dbg_ab", [128, 2, PC], F32, kind="ExternalOutput").ap()
        dbg_r = nc.dram_tensor("dbg_r", [128, PC], F32, kind="ExternalOutput").ap()
        dbg_krep = nc.dram_tensor("dbg_krep", [128, 64, SC], BF16, kind="ExternalOutput").ap()
        dbg_o = nc.dram_tensor("dbg_o", [128, 8, SC], BF16, kind="ExternalOutput").ap()
        dbg_y = nc.dram_tensor("dbg_y", [128, 8, PC], BF16, kind="ExternalOutput").ap()

    alloc_common(p)
    lnbuf = alloc_lnbuf(p)
    stage = lnbuf["r"]
    xall = p.sbuf("xall", [128, NT, D], BF16)
    xTb = [p.sbuf(f"xTb{i}", [128, 8, PC + 3], BF16) for i in range(2)]
    wtile = [p.sbuf(f"wtile{i}", [128, D], BF16) for i in range(2)]
    cbt = p.sbuf("cbt", [128, 8, 4], F32)
    barep = p.sbuf("barep_s", [128, 2], F32)
    waB = p.sbuf("waB", [128, 8, 128], BF16)
    wbB = p.sbuf("wbB", [128, 8, 128], BF16)
    oneblk2 = p.sbuf("oneblk2_s", [128, 2], BF16)
    qT = p.sbuf("qT", [128, 8, PC], BF16)
    kT = p.sbuf("kT", [128, 8, PC], BF16)
    vT = p.sbuf("vT", [128, 8, PC], BF16)
    gT = p.sbuf("gT", [128, 8, PC], BF16)
    yTb = p.sbuf("yTb", [128, 8, PC], BF16)
    ksq = p.sbuf("ksq", [128, 1024], BF16)
    nrm = p.sbuf("nrm", [128, 64], F32)
    rtok = p.sbuf("rtok", [128, 64], F32)
    rtokb = p.sbuf("rtokb", [128, 64], BF16)
    rtT = p.sbuf("rtT", [128, 128], BF16)
    rpad = p.sbuf("rpad", [128, 128], BF16)
    repsel = p.sbuf("repsel_s", [16, 128], BF16)
    a_rep = p.sbuf("a_rep", [128, PC], F32)
    b_rep = p.sbuf("b_rep", [128, PC], F32)
    r_rep = p.sbuf("r_rep", [128, PC], F32)
    br_rep = p.sbuf("br_rep", [128, PC], F32)
    s1_rep = p.sbuf("s1_rep", [128, PC], F32)
    tmpP = p.sbuf("tmpP", [128, PC], F32)
    k_rep = p.sbuf("k_rep", [128, SC, 64], BF16)
    q_rep = p.sbuf("q_rep", [128, SC, 64], BF16)
    v_scan = p.sbuf("v_scan", [128, 8, SC], BF16)
    vbr = p.sbuf("vbr", [128, 8, SC], F32)
    o_chunk = p.sbuf("o_chunk", [128, 8, SC], BF16)
    S_ = p.sbuf("S_", [128, 8, 64], BF16)
    Ma = p.sbuf("Ma", [128, 4, 64], BF16)
    Mb = p.sbuf("Mb", [128, 4, 64], BF16)
    T1a = p.sbuf("T1a", [128, 4, 64], BF16)
    T1b = p.sbuf("T1b", [128, 4, 64], BF16)
    Spa = p.sbuf("Spa", [128, 4, 64], BF16)
    Spb = p.sbuf("Spb", [128, 4, 64], BF16)
    da = p.sbuf("da", [128, 4], F32)
    db = p.sbuf("db", [128, 4], F32)
    u2a = p.sbuf("u2a", [128, 4], BF16)
    u2b = p.sbuf("u2b", [128, 4], BF16)
    qb_row = p.sbuf("qb_row", [1, D], BF16)
    kb_row = p.sbuf("kb_row", [1, D], BF16)
    yout = p.sbuf("yout_l1", [128, D], F32)

    ps8 = [p.psum(f"ps8_{i}", [128, 512], F32) for i in range(8)]

    blk = p.es.enter_context(nc.Block())
    load_consts(p, ident_h, ones1_h, onesf_h, onescol_h, onesrf_h)
    for dst, srch in [(cbt[:], cb_h), (barep[:], barep_h), (oneblk2[:], oneblk2_h),
                      (repsel[:], repsel_h)]:
        nc.sync.dma_start(out=dst, in_=srch).then_inc(p.sld, 16)
        p.inc("sld", 16)
    for part in range(12):
        nc.sync.wait_ge(p.sve, p.c["sve"])
        nc.sync.dma_start(out=stage[0:1, :], in_=cw_h[part]).then_inc(p.sld, 16)
        p.inc("sld", 16)
        nc.tensor.wait_ge(p.sld, p.c["sld"])
        if part > 0:
            nc.tensor.wait_ge(p.sve, p.c["sve"])
        nc.tensor.matmul(ps8[0][:], p.onesrf[:], stage[0:1, 0:512], start=True, stop=True)
        ins = nc.tensor.matmul(ps8[1][:], p.onesrf[:], stage[0:1, 512:1024], start=True, stop=True)
        ins.then_inc(p.smm, 1); p.inc("smm")
        nc.vector.wait_ge(p.smm, p.c["smm"])
        nc.vector.tensor_copy(xall[:, part, 0:512], ps8[0][:])
        ins = nc.vector.tensor_copy(xall[:, part, 512:1024], ps8[1][:])
        ins.then_inc(p.sve, 1); p.inc("sve")
    for s in range(8):
        nc.sync.wait_ge(p.sve, p.c["sve"])
        nc.sync.dma_start(out=stage[:, 0:128], in_=waT_h[s]).then_inc(p.sld, 16); p.inc("sld", 16)
        nc.sync.dma_start(out=stage[:, 128:256], in_=wbT_h[s]).then_inc(p.sld, 16); p.inc("sld", 16)
        nc.vector.wait_ge(p.sld, p.c["sld"])
        nc.vector.tensor_copy(waB[:, s, :], stage[:, 0:128])
        ins = nc.vector.tensor_copy(wbB[:, s, :], stage[:, 128:256])
        ins.then_inc(p.sve, 1); p.inc("sve")

    nc.sync.wait_ge(p.sve, p.c["sve"])
    nc.sync.dma_start(out=stage[0:1, :], in_=cbrow_h[0:1, :]).then_inc(p.sld, 16); p.inc("sld", 16)
    nc.scalar.wait_ge(p.sld, p.c["sld"])
    ins = nc.scalar.activation(qb_row[:], stage[0:1, :], AF.Copy)
    ins.then_inc(p.ssc, 1); p.inc("ssc")
    nc.sync.wait_ge(p.ssc, p.c["ssc"])
    nc.sync.dma_start(out=stage[0:1, :], in_=cbrow_h[1:2, :]).then_inc(p.sld, 16); p.inc("sld", 16)
    nc.scalar.wait_ge(p.sld, p.c["sld"])
    ins = nc.scalar.activation(kb_row[:], stage[0:1, :], AF.Copy)
    ins.then_inc(p.ssc, 1); p.inc("ssc")

    def prep_stacked(w_h, w_scr, cw0):
        for s in range(8):
            nc.sync.wait_ge(p.sve, p.c["sve"])
            nc.sync.wait_ge(p.sst, p.c["sst"])
            nc.sync.dma_start(out=stage[:], in_=w_h[s]).then_inc(p.sld, 16); p.inc("sld", 16)
            nc.vector.wait_ge(p.sld, p.c["sld"])
            for kk in range(4):
                if kk >= 2:
                    nc.vector.wait_ge(p.sst, p.c["sst"] - 16)
                ins = nc.vector.tensor_tensor(wtile[kk % 2][:], stage[:],
                                              xall[:, cw0 + kk, :], op=ALU.mult)
                ins.then_inc(p.sve, 1); p.inc("sve")
                nc.gpsimd.wait_ge(p.sve, p.c["sve"])
                nc.gpsimd.dma_start(out=w_scr[kk, s], in_=wtile[kk % 2][:])\
                    .then_inc(p.sst, 16)
                p.inc("sst", 16)

    prep_stacked(wqT_h, wq_s, 0)
    prep_stacked(wkT_h, wk_s, 4)
    prep_stacked(wvT_h, wv_s, 8)

    def prep_plain(w_h, w_scr):
        for s in range(8):
            nc.sync.wait_ge(p.sve, p.c["sve"])
            nc.sync.wait_ge(p.sst, p.c["sst"])
            nc.sync.dma_start(out=stage[:], in_=w_h[s]).then_inc(p.sld, 16); p.inc("sld", 16)
            nc.vector.wait_ge(p.sld, p.c["sld"])
            ins = nc.vector.tensor_copy(wtile[s % 2][:], stage[:])
            ins.then_inc(p.sve, 1); p.inc("sve")
            nc.gpsimd.wait_ge(p.sve, p.c["sve"])
            nc.gpsimd.dma_start(out=w_scr[s], in_=wtile[s % 2][:]).then_inc(p.sst, 16)
            p.inc("sst", 16)

    prep_plain(wgT_h, wg_s)
    prep_plain(woT_h, wo_s)

    nc.vector.memset(xTb[0][:], 0.0)
    nc.vector.memset(xTb[1][:], 0.0)
    ins = nc.vector.memset(S_[:], 0.0)
    ins.then_inc(p.sve, 1); p.inc("sve")
    nc.sync.wait_ge(p.sve, p.c["sve"])
    nc.sync.wait_ge(p.sst, p.c["sst"])
    for tl in range(NT):
        nc.sync.wait_ge(p.ssc, p.c["ssc"])
        nc.sync.dma_start(out=stage[:], in_=x_in[tl * 128:(tl + 1) * 128, :])\
            .then_inc(p.sld, 16)
        p.inc("sld", 16)
        nc.scalar.wait_ge(p.sld, p.c["sld"])
        ins = nc.scalar.activation(xall[:, tl, :], stage[:], AF.Copy)
        ins.then_inc(p.ssc, 1); p.inc("ssc")
    all_loaded = p.c["ssc"]

    for pc in range(NPC):
        xb = xTb[pc % 2]
        xprev = xTb[1 - pc % 2]
        nc.sync.wait_ge(p.ssc, all_loaded)
        if pc > 0:
            nc.sync.wait_ge(p.smm, p.c["smm"])
        for tl4 in range(PC // 128):
            tl = pc * (PC // 128) + tl4
            for ds in range(8):
                nc.sync.dma_start(out=xb[:, ds, 3 + tl4 * 128:3 + (tl4 + 1) * 128],
                                  in_=xall[:, tl, ds * 128:(ds + 1) * 128],
                                  transpose=True).then_inc(p.sld, 16)
                p.inc("sld", 16)
        if pc > 0:
            nc.gpsimd.wait_ge(p.smm, p.c["smm"])
            nc.gpsimd.dma_start(out=xb[:, :, 0:3], in_=xprev[:, :, PC:PC + 3])\
                .then_inc(p.sgp, 16)
            p.inc("sgp", 16)
            nc.tensor.wait_ge(p.sgp, p.c["sgp"])
        xT_ld = p.c["sld"]
        nc.tensor.wait_ge(p.sld, xT_ld)

        def proj_stacked(w_scr, dstT, cb_col):
            for kk in range(4):
                for s in range(8):
                    wt = wtile[(kk * 8 + s) % 2]
                    if kk * 8 + s >= 2:
                        nc.sync.wait_ge(p.smm, p.c["smm"] - 1)
                    nc.sync.dma_start(out=wt[:], in_=w_scr[kk, s]).then_inc(p.sld, 16)
                    p.inc("sld", 16)
                    nc.tensor.wait_ge(p.sld, p.c["sld"])
                    if kk == 0 and s == 0:
                        nc.tensor.wait_ge(p.ssc, p.c["ssc"])
                    for cs in range(8):
                        ins = nc.tensor.matmul(
                            ps8[cs][:], wt[:, cs * 128:(cs + 1) * 128],
                            xb[:, s, kk:kk + PC], start=(kk == 0 and s == 0),
                            stop=(kk == 3 and s == 7))
                    ins.then_inc(p.smm, 1); p.inc("smm")
            nc.scalar.wait_ge(p.smm, p.c["smm"])
            for cs in range(8):
                ins = nc.scalar.activation(dstT[:, cs, :], ps8[cs][:], AF.Silu,
                                           bias=cbt[:, cs, cb_col:cb_col + 1], scale=1.0)
            ins.then_inc(p.ssc, 1); p.inc("ssc")

        q_tok = qT[:].rearrange("p c t -> p (c t)").rearrange("p (a b) -> p a b", b=1024)
        k_tok = kT[:].rearrange("p c t -> p (c t)").rearrange("p (a b) -> p a b", b=1024)

        def proj_stacked_tok(w_scr, dst_tok, bias_row):
            for kk in range(4):
                for s in range(8):
                    wt = wtile[(kk * 8 + s) % 2]
                    if kk * 8 + s >= 2:
                        nc.sync.wait_ge(p.smm, p.c["smm"] - 1)
                    nc.sync.dma_start(out=wt[:], in_=w_scr[kk, s]).then_inc(p.sld, 16)
                    p.inc("sld", 16)
                    nc.tensor.wait_ge(p.sld, p.c["sld"])
                    if kk == 0 and s == 0:
                        nc.tensor.wait_ge(p.ssc, p.c["ssc"])
                    for tl4 in range(4):
                        for half in range(2):
                            ins = nc.tensor.matmul(
                                ps8[tl4 * 2 + half][:],
                                xb[:, s, kk + tl4 * 128:kk + tl4 * 128 + 128],
                                wt[:, half * 512:(half + 1) * 512],
                                start=(kk == 0 and s == 0), stop=False)
            for tl4 in range(4):
                for half in range(2):
                    ins = nc.tensor.matmul(ps8[tl4 * 2 + half][:], p.ones1[:],
                                           bias_row[:, half * 512:(half + 1) * 512],
                                           start=False, stop=True)
            ins.then_inc(p.smm, 1); p.inc("smm")
            nc.scalar.wait_ge(p.smm, p.c["smm"])
            for tl4 in range(4):
                for half in range(2):
                    ins = nc.scalar.activation(
                        dst_tok[:, tl4, half * 512:(half + 1) * 512],
                        ps8[tl4 * 2 + half][:], AF.Silu)
            ins.then_inc(p.ssc, 1); p.inc("ssc")

        proj_stacked_tok(wq_s, q_tok, qb_row[:])
        proj_stacked_tok(wk_s, k_tok, kb_row[:])
        proj_stacked(wv_s, vT, 2)

        for s in range(8):
            wt = wtile[s % 2]
            if s >= 2:
                nc.sync.wait_ge(p.smm, p.c["smm"] - 1)
            nc.sync.dma_start(out=wt[:], in_=wg_s[s]).then_inc(p.sld, 16)
            p.inc("sld", 16)
            nc.tensor.wait_ge(p.sld, p.c["sld"])
            if s == 0:
                nc.tensor.wait_ge(p.ssc, p.c["ssc"])
            for cs in range(8):
                ins = nc.tensor.matmul(ps8[cs][:], wt[:, cs * 128:(cs + 1) * 128],
                                       xb[:, s, 3:3 + PC], start=(s == 0), stop=(s == 7))
            ins.then_inc(p.smm, 1); p.inc("smm")
        nc.scalar.wait_ge(p.smm, p.c["smm"])
        for cs in range(8):
            ins = nc.scalar.activation(gT[:, cs, :], ps8[cs][:], AF.Silu,
                                       bias=cbt[:, cs, 3:4], scale=1.0)
        ins.then_inc(p.ssc, 1); p.inc("ssc")
        nc.tensor.wait_ge(p.ssc, p.c["ssc"])
        for s in range(8):
            nc.tensor.matmul(ps8[0][:], waB[:, s, :], xb[:, s, 3:3 + PC],
                             start=(s == 0), stop=(s == 7))
            ins = nc.tensor.matmul(ps8[1][:], wbB[:, s, :], xb[:, s, 3:3 + PC],
                                   start=(s == 0), stop=(s == 7))
        ins.then_inc(p.smm, 1); p.inc("smm")
        nc.scalar.wait_ge(p.smm, p.c["smm"])
        nc.scalar.activation(a_rep[:], ps8[0][:], AF.Sigmoid, bias=barep[:, 0:1], scale=1.0)
        ins = nc.scalar.activation(b_rep[:], ps8[1][:], AF.Sigmoid, bias=barep[:, 1:2], scale=1.0)
        ins.then_inc(p.ssc, 1); p.inc("ssc")
        ab_done = p.c["ssc"]
        for tl4 in range(4):
            nc.scalar.wait_ge(p.ssc, p.c["ssc"])
            if tl4 > 0:
                nc.scalar.wait_ge(p.sve, p.c["sve"])
            ins = nc.scalar.activation(ksq[:, 0:1024], k_tok[:, tl4, :], AF.Square)
            ins.then_inc(p.ssc, 1); p.inc("ssc")
            nc.vector.wait_ge(p.ssc, p.c["ssc"])
            ins = nc.vector.tensor_reduce(
                nrm[:, tl4 * 16:(tl4 + 1) * 16].unsqueeze(-1),
                ksq[:, 0:1024].rearrange("p (h j) -> p h j", j=64),
                axis=AX.X, op=ALU.add)
            ins.then_inc(p.sve, 1); p.inc("sve")
        nc.scalar.wait_ge(p.sve, p.c["sve"])
        ins = nc.scalar.activation(nrm[:], nrm[:], AF.Sqrt)
        ins.then_inc(p.ssc, 1); p.inc("ssc")
        nc.vector.wait_ge(p.ssc, p.c["ssc"])
        nc.vector.tensor_scalar_max(rtok[:], nrm[:], 1e-12)
        spacer(nc, p)
        nc.vector.reciprocal(rtok[:], rtok[:])
        spacer(nc, p)
        ins = nc.vector.tensor_copy(rtokb[:], rtok[:])
        ins.then_inc(p.sve, 1); p.inc("sve")
        for tl4 in range(4):
            nc.vector.wait_ge(p.sld, p.c["sld"])
            ins = nc.vector.tensor_copy(rpad[:, 0:16], rtokb[:, tl4 * 16:(tl4 + 1) * 16])
            ins.then_inc(p.sve, 1); p.inc("sve")
            nc.sync.wait_ge(p.sve, p.c["sve"])
            nc.sync.dma_start(out=rtT[:], in_=rpad[:], transpose=True).then_inc(p.sld, 16)
            p.inc("sld", 16)
            nc.tensor.wait_ge(p.sld, p.c["sld"])
            ins = nc.tensor.matmul(ps8[3][:, 0:128],
                                   repsel[:], rtT[0:16, :], start=True, stop=True)
            ins.then_inc(p.smm, 1); p.inc("smm")
            nc.vector.wait_ge(p.smm, p.c["smm"])
            ins = nc.vector.tensor_copy(r_rep[:, tl4 * 128:(tl4 + 1) * 128], ps8[3][:, 0:128])
            ins.then_inc(p.sve, 1); p.inc("sve")
        nc.vector.wait_ge(p.ssc, ab_done)
        nc.vector.tensor_tensor(br_rep[:], b_rep[:], r_rep[:], op=ALU.mult)
        spacer(nc, p)
        nc.vector.tensor_tensor(tmpP[:], br_rep[:], r_rep[:], op=ALU.mult)
        spacer(nc, p)
        nc.vector.tensor_tensor(s1_rep[:], tmpP[:], a_rep[:], op=ALU.mult)
        spacer(nc, p)
        ins = nc.vector.tensor_scalar_mul(s1_rep[:], s1_rep[:], -1.0)
        ins.then_inc(p.sve, 1); p.inc("sve")
        scal_done = p.c["sve"]

        for sc in range(NSC):
            t0 = sc * SC
            nc.gpsimd.wait_ge(p.ssc, ab_done)
            if pc + sc > 0:
                nc.gpsimd.wait_ge(p.sve, p.c["sve"])
            sc_tl0 = sc
            qs_src = q_tok[:, sc_tl0, :].rearrange("p (h j) -> p h j", j=64)
            ks_src = k_tok[:, sc_tl0, :].rearrange("p (h j) -> p h j", j=64)
            sdst = lambda scr: scr.rearrange("(h t j) -> t h j", h=16, t=SC)
            nc.gpsimd.dma_start(out=sdst(kscr), in_=ks_src).then_inc(p.sgp, 16)
            p.inc("sgp", 16)
            nc.gpsimd.dma_start(out=sdst(qscr), in_=qs_src).then_inc(p.sgp, 16)
            p.inc("sgp", 16)
            vv = vT[:, :, t0:t0 + SC]
            nc.gpsimd.dma_start(
                out=vscr.rearrange("(c pp t) -> pp c t", c=8, pp=128), in_=vv)\
                .then_inc(p.sgp, 16)
            p.inc("sgp", 16)
            nc.sync.wait_ge(p.sgp, p.c["sgp"])
            for h in range(16):
                base = h * 64 * SC
                srcb = kscr[base:base + 64 * SC].rearrange("(t j) -> t j", j=64)\
                    .unsqueeze(0).broadcast_to([8, SC, 64])
                nc.sync.dma_start(out=k_rep[h * 8:(h + 1) * 8, :, :], in_=srcb)\
                    .then_inc(p.sld, 16)
                p.inc("sld", 16)
                srcb = qscr[base:base + 64 * SC].rearrange("(t j) -> t j", j=64)\
                    .unsqueeze(0).broadcast_to([8, SC, 64])
                nc.sync.dma_start(out=q_rep[h * 8:(h + 1) * 8, :, :], in_=srcb)\
                    .then_inc(p.sld, 16)
                p.inc("sld", 16)
            nc.sync.dma_start(out=v_scan[:],
                              in_=vscr.rearrange("(pp i t) -> pp i t", pp=128, i=8))\
                .then_inc(p.sld, 16)
            p.inc("sld", 16)
            scan_ld = p.c["sld"]
            nc.vector.wait_ge(p.sld, scan_ld)
            nc.vector.wait_ge(p.sve, scal_done)
            brb = br_rep[:, t0:t0 + SC].unsqueeze(1).broadcast_to([128, 8, SC])
            ins = nc.vector.tensor_tensor(vbr[:], v_scan[:], brb, op=ALU.mult)
            ins.then_inc(p.sve, 1); p.inc("sve")
            for t in range(SC):
                k_b = k_rep[:, t, :].unsqueeze(1).broadcast_to([128, 4, 64])
                q_b = q_rep[:, t, :].unsqueeze(1).broadcast_to([128, 4, 64])
                at = a_rep[:, t0 + t:t0 + t + 1]
                s1t = s1_rep[:, t0 + t:t0 + t + 1]
                nc.vector.tensor_tensor(Ma[:], S_[:, 0:4, :], k_b, op=ALU.mult)
                nc.vector.tensor_tensor(Mb[:], S_[:, 4:8, :], k_b, op=ALU.mult)
                nc.vector.tensor_reduce(da[:].unsqueeze(-1), Ma[:], axis=AX.X, op=ALU.add)
                nc.vector.tensor_reduce(db[:].unsqueeze(-1), Mb[:], axis=AX.X, op=ALU.add)
                nc.vector.scalar_tensor_tensor(u2a[:], da[:], s1t, vbr[:, 0:4, t],
                                               op0=ALU.mult, op1=ALU.add)
                nc.vector.scalar_tensor_tensor(u2b[:], db[:], s1t, vbr[:, 4:8, t],
                                               op0=ALU.mult, op1=ALU.add)
                spacer(nc, p)
                nc.vector.tensor_tensor(T1a[:], u2a[:].unsqueeze(-1).broadcast_to([128, 4, 64]),
                                        k_b, op=ALU.mult)
                nc.vector.tensor_tensor(T1b[:], u2b[:].unsqueeze(-1).broadcast_to([128, 4, 64]),
                                        k_b, op=ALU.mult)
                nc.vector.scalar_tensor_tensor(Spa[:], S_[:, 0:4, :], at, T1a[:],
                                               op0=ALU.mult, op1=ALU.add)
                nc.vector.scalar_tensor_tensor(Spb[:], S_[:, 4:8, :], at, T1b[:],
                                               op0=ALU.mult, op1=ALU.add)
                nc.vector.tensor_scalar(S_[:, 0:4, :], Spa[:], 5.0, -5.0,
                                        op0=ALU.min, op1=ALU.max)
                nc.vector.tensor_scalar(S_[:, 4:8, :], Spb[:], 5.0, -5.0,
                                        op0=ALU.min, op1=ALU.max)
                nc.vector.tensor_tensor(Ma[:], S_[:, 0:4, :], q_b, op=ALU.mult)
                nc.vector.tensor_tensor(Mb[:], S_[:, 4:8, :], q_b, op=ALU.mult)
                with nc.allow_low_precision(reason="DVE reduce accumulates fp32 internally; bf16 store only"):
                    nc.vector.tensor_reduce(o_chunk[:, 0:4, t:t + 1], Ma[:], axis=AX.X, op=ALU.add)
                    ins = nc.vector.tensor_reduce(o_chunk[:, 4:8, t:t + 1], Mb[:], axis=AX.X, op=ALU.add)
            ins.then_inc(p.sve, 1); p.inc("sve")
            nc.gpsimd.wait_ge(p.sve, p.c["sve"])
            nc.gpsimd.dma_start(out=oscr.rearrange("(pp i t) -> pp i t", pp=128, i=8),
                                in_=o_chunk[:]).then_inc(p.sgp, 16)
            p.inc("sgp", 16)
            nc.sync.wait_ge(p.sgp, p.c["sgp"])
            nc.sync.dma_start(out=yTb[:, :, t0:t0 + SC],
                              in_=oscr.rearrange("(c pp t) -> pp c t", c=8, pp=128))\
                .then_inc(p.sld, 16)
            p.inc("sld", 16)
        scan_done_ld = p.c["sld"]
        nc.vector.wait_ge(p.sld, scan_done_ld)
        for cs in range(8):
            ins = nc.vector.tensor_tensor(yTb[:, cs, :], yTb[:, cs, :], gT[:, cs, :],
                                          op=ALU.mult)
        ins.then_inc(p.sve, 1); p.inc("sve")
        y_done = p.c["sve"]
        for s in range(8):
            wt = wtile[s % 2]
            if s >= 2:
                nc.sync.wait_ge(p.smm, p.c["smm"] - 1)
            nc.sync.dma_start(out=wt[:], in_=wo_s[s]).then_inc(p.sld, 16)
            p.inc("sld", 16)
            nc.tensor.wait_ge(p.sld, p.c["sld"])
            if s == 0:
                nc.tensor.wait_ge(p.sve, y_done)
                nc.tensor.wait_ge(p.ssc, p.c["ssc"])
            for tl4 in range(4):
                for dc in range(2):
                    ins = nc.tensor.matmul(
                        ps8[tl4 * 2 + dc][:],
                        yTb[:, s, tl4 * 128:(tl4 + 1) * 128],
                        wt[:, dc * 512:(dc + 1) * 512],
                        start=(s == 0), stop=(s == 7))
            ins.then_inc(p.smm, 1); p.inc("smm")
        nc.vector.wait_ge(p.smm, p.c["smm"])
        for tl4 in range(4):
            tl = pc * 4 + tl4
            emit_ln(p, ps8[tl4 * 2][:], ps8[tl4 * 2 + 1][:],
                    xall[:, tl, :], lnbuf, yout[:], 1e-5)
            nc.gpsimd.wait_ge(p.sve, p.c["sve"])
            nc.gpsimd.dma_start(out=x_out[tl * 128:(tl + 1) * 128, :], in_=yout[:])\
                .then_inc(p.sst, 16)
            p.inc("sst", 16)
            nc.vector.wait_ge(p.sst, p.c["sst"])
    nc.sync.wait_ge(p.sst, p.c["sst"])
    return p


_PROGS = {}

def _get_prog(name):
    if name not in _PROGS:
        if name == "l0attn":
            pp = build_l0attn(); _PROGS[name] = pp.close()
        elif name == "l0ffn":
            pp = build_l0ffn(); _PROGS[name] = pp.close()
        elif name == "l1attn":
            pp = build_l1attn(); _PROGS[name] = pp.close()
        elif name == "l1ffn":
            pp = build_l1ffn(); _PROGS[name] = pp.close()
    return _PROGS[name]


def _consts():
    c = {
        "ident": np.eye(128, dtype=ml_dtypes.bfloat16),
        "ones1": np.ones((1, 128), dtype=ml_dtypes.bfloat16),
        "onesf": np.ones((128, 1), dtype=np.float32),
        "onescol": np.ones((128, 1), dtype=ml_dtypes.bfloat16),
        "onesrf": np.ones((1, 128), dtype=np.float32),
    }
    return c


def kernel(
    L0_attn_x, L0_attn_W_pe, L0_attn_b_pe, L0_attn_stt_energy, L0_attn_W_fc,
    L0_attn_gamma, L0_attn_beta, L0_ffn_W1, L0_ffn_b1, L0_ffn_W2, L0_ffn_b2,
    L1_attn_Wq, L1_attn_Wk, L1_attn_Wv, L1_attn_Wa, L1_attn_ba, L1_attn_Wb,
    L1_attn_bb, L1_attn_conv_q_w, L1_attn_conv_q_b, L1_attn_conv_k_w,
    L1_attn_conv_k_b, L1_attn_conv_v_w, L1_attn_conv_v_b, L1_attn_Wg,
    L1_attn_bg, L1_attn_Wo, L1_ffn_w_gate, L1_ffn_w_up, L1_ffn_w_down,
):
    B = 8
    x = np.ascontiguousarray(np.asarray(L0_attn_x, dtype=np.float32))
    C = _consts()

    W_pe = np.asarray(L0_attn_W_pe, np.float32)
    b_pe = np.asarray(L0_attn_b_pe, np.float32)
    stt = np.asarray(L0_attn_stt_energy, np.float32).reshape(16, 31)
    W_fc = np.asarray(L0_attn_W_fc, np.float32)
    gamma = np.asarray(L0_attn_gamma, np.float32)
    beta = np.asarray(L0_attn_beta, np.float32)
    wpe_blk = np.zeros((128, 124), np.float32)
    wpe_blk[0:64, 0:62] = W_pe
    wpe_blk[64:128, 62:124] = W_pe
    bpe_row = np.zeros((1, 1024), np.float32)
    for s in range(8):
        for hp in range(2):
            bpe_row[0, s * 128 + hp * 62:s * 128 + hp * 62 + 62] = b_pe
    stt_row = np.ascontiguousarray(stt.T).reshape(1, 496)
    recip = np.zeros((128, 32), np.float32)
    for tl in range(32):
        recip[:, tl] = 1.0 / (tl * 128 + np.arange(128) + 1)
    mask0 = np.zeros((128, 496), np.float32)
    for t in range(128):
        for k in range(31):
            if t + k >= 30:
                mask0[t, k * 16 + np.arange(16)] = 1.0
    ibig = np.zeros((128, 260), np.float32)
    for ppp in range(128):
        if ppp + 128 < 260:
            ibig[ppp, ppp + 128] = 1.0
    jbig = np.zeros((30, 286), np.float32)
    for jj in range(30):
        jbig[jj, jj + 128] = 1.0
    m0 = {
        "tri": np.triu(np.ones((128, 128))).astype(ml_dtypes.bfloat16),
        "wpe": wpe_blk, "bpe": bpe_row, "stt": stt_row, "recip": recip,
        "mask0": mask0,
        "wfcT": np.ascontiguousarray(W_fc.T).reshape(8, 128, 1024),
        "gam": gamma.reshape(1, 1024), "bet": beta.reshape(1, 1024),
        "ibig": ibig.astype(ml_dtypes.bfloat16),
        "jbig": jbig.astype(ml_dtypes.bfloat16),
    }
    m0.update(C)
    nc0 = _get_prog("l0attn")
    maps = [dict(m0, x_in=np.ascontiguousarray(x[b])) for b in range(B)]
    res = run_bass_kernel_spmd(nc0, maps, core_ids=list(range(B)))
    x1 = [res.results[b]["x_out"] for b in range(B)]

    W1 = np.asarray(L0_ffn_W1, np.float32)
    b1 = np.asarray(L0_ffn_b1, np.float32)
    W2 = np.asarray(L0_ffn_W2, np.float32)
    b2 = np.asarray(L0_ffn_b2, np.float32)
    m1 = {
        "w1t": np.ascontiguousarray(W1.T).reshape(8, 128, 4096),
        "b1a": np.ascontiguousarray(b1.reshape(32, 128).T),
        "w2t": np.ascontiguousarray(W2.T).reshape(32, 128, 1024),
        "b2r": b2.reshape(1, 1024),
    }
    m1.update(C)
    nc1 = _get_prog("l0ffn")
    maps = [dict(m1, x_in=np.ascontiguousarray(x1[b])) for b in range(B)]
    res = run_bass_kernel_spmd(nc1, maps, core_ids=list(range(B)))
    x2 = [res.results[b]["x_out"] for b in range(B)]

    Wq = np.asarray(L1_attn_Wq, np.float32); Wk = np.asarray(L1_attn_Wk, np.float32)
    Wv = np.asarray(L1_attn_Wv, np.float32); Wg = np.asarray(L1_attn_Wg, np.float32)
    Wo = np.asarray(L1_attn_Wo, np.float32)
    Wa = np.asarray(L1_attn_Wa, np.float32); Wb = np.asarray(L1_attn_Wb, np.float32)
    ba = np.asarray(L1_attn_ba, np.float32); bb = np.asarray(L1_attn_bb, np.float32)
    cqw = np.asarray(L1_attn_conv_q_w, np.float32); cqb = np.asarray(L1_attn_conv_q_b, np.float32)
    ckw = np.asarray(L1_attn_conv_k_w, np.float32); ckb = np.asarray(L1_attn_conv_k_b, np.float32)
    cvw = np.asarray(L1_attn_conv_v_w, np.float32); cvb = np.asarray(L1_attn_conv_v_b, np.float32)
    bg = np.asarray(L1_attn_bg, np.float32)

    def wT8(W):
        return np.ascontiguousarray(W.T).reshape(8, 128, 1024)
    waT = np.zeros((1024, 128), np.float32)
    wbT = np.zeros((1024, 128), np.float32)
    for h in range(16):
        for i8 in range(8):
            waT[:, h * 8 + i8] = Wa[h]
            wbT[:, h * 8 + i8] = Wb[h]
    cw = np.zeros((12, 1, 1024), np.float32)
    for j, w3 in enumerate([cqw, ckw, cvw]):
        for kk in range(4):
            cw[j * 4 + kk, 0, :] = w3[:, 0, kk]
    cb = np.zeros((128, 8, 4), np.float32)
    for ss in range(8):
        cb[:, ss, 0] = cqb[ss * 128:(ss + 1) * 128]
        cb[:, ss, 1] = ckb[ss * 128:(ss + 1) * 128]
        cb[:, ss, 2] = cvb[ss * 128:(ss + 1) * 128]
        cb[:, ss, 3] = bg[ss * 128:(ss + 1) * 128]
    barep = np.zeros((128, 2), np.float32)
    barep[:, 0] = np.repeat(ba, 8)
    barep[:, 1] = np.repeat(bb, 8)
    oneblk2 = np.zeros((128, 2), np.float32)
    oneblk2[0:64, 0] = 1.0
    oneblk2[64:128, 1] = 1.0
    repsel = np.zeros((16, 128), np.float32)
    for h in range(16):
        repsel[h, h * 8:(h + 1) * 8] = 1.0
    m2 = {
        "wqT": wT8(Wq), "wkT": wT8(Wk), "wvT": wT8(Wv), "wgT": wT8(Wg), "woT": wT8(Wo),
        "waT": np.ascontiguousarray(waT).reshape(8, 128, 128),
        "wbT": np.ascontiguousarray(wbT).reshape(8, 128, 128),
        "cw": cw, "cb": cb, "barep": barep,
        "cbrow": np.stack([cqb, ckb], axis=0),
        "oneblk2": oneblk2.astype(ml_dtypes.bfloat16),
        "repsel": repsel.astype(ml_dtypes.bfloat16),
    }
    m2.update(C)
    nc2 = _get_prog("l1attn")
    maps = [dict(m2, x_in=np.ascontiguousarray(x2[b])) for b in range(B)]
    res = run_bass_kernel_spmd(nc2, maps, core_ids=list(range(B)))
    x3 = [res.results[b]["x_out"] for b in range(B)]

    wg_i = np.asarray(L1_ffn_w_gate, np.int32)
    wu_i = np.asarray(L1_ffn_w_up, np.int32)
    wd_i = np.asarray(L1_ffn_w_down, np.int32)
    m3 = {
        "wgT": np.ascontiguousarray(wg_i.T).reshape(8, 128, 3072),
        "wuT": np.ascontiguousarray(wu_i.T).reshape(8, 128, 3072),
        "wdT": np.ascontiguousarray(wd_i.T).reshape(24, 128, 1024),
    }
    m3.update(C)
    nc3 = _get_prog("l1ffn")
    maps = [dict(m3, x_in=np.ascontiguousarray(x3[b])) for b in range(B)]
    res = run_bass_kernel_spmd(nc3, maps, core_ids=list(range(B)))
    out = np.stack([res.results[b]["x_out"] for b in range(B)], axis=0)
    return out.astype(np.float32)
